# version 26
# baseline (speedup 1.0000x reference)
import numpy as np
import ml_dtypes
import concourse.bass as bass
import concourse.mybir as mybir
from concourse.alu_op_type import AluOpType as ALU
from concourse.bass_utils import run_bass_kernel_spmd
from contextlib import ExitStack

F32 = mybir.dt.float32
BF16 = mybir.dt.bfloat16
I32 = mybir.dt.int32
AF = mybir.ActivationFunctionType
AXX = mybir.AxisListType.X
BF = ml_dtypes.bfloat16

NCORE = 8
D = 4096
TPC = 1024
NT = TPC // 128
KC = D // 128
OSW = 512
NOS = D // OSW
EPS = 1e-5
MAXTHR = 0.125
SCALING = 2.0


class Sem:
    __slots__ = ("h", "n")

    def __init__(self, h):
        self.h = h
        self.n = 0


def build_nc():
    nc = bass.Bass("TRN2", target_bir_lowering=False)

    xin = nc.dram_tensor("xin", [TPC, D], F32, kind="ExternalInput")
    wmT = nc.dram_tensor("wmT", [D, D], F32, kind="ExternalInput")
    wbT = nc.dram_tensor("wbT", [D, D], F32, kind="ExternalInput")
    maskN = nc.dram_tensor("maskN", [D // 8, D], I32, kind="ExternalInput")
    btin = nc.dram_tensor("btin", [128, 16], F32)
    btout = nc.dram_tensor("btout", [1024, 16], F32)
    ga = nc.dram_tensor("ga", [128, KC * 144], F32, kind="ExternalInput")
    gab = nc.dram_tensor("gab", [1, 144], F32, kind="ExternalInput")
    bcat = nc.dram_tensor("bcat", [128, D], F32, kind="ExternalInput")
    battn = nc.dram_tensor("battn", [1, D], F32, kind="ExternalInput")
    bbase = nc.dram_tensor("bbase", [1, D], F32, kind="ExternalInput")
    selp = nc.dram_tensor("selp", [128, KC * 128], BF16, kind="ExternalInput")
    idnp = nc.dram_tensor("idnp", [128, 128], BF16, kind="ExternalInput")
    x2d = nc.dram_tensor("x2d", [TPC, D], F32)
    a_dram = nc.dram_tensor("a_dram", [TPC, D], BF16)
    f_dram = nc.dram_tensor("f_dram", [TPC, D], BF16)
    outd = nc.dram_tensor("outd", [TPC, D], F32, kind="ExternalOutput")

    es = ExitStack()
    sb = lambda n, s, d: es.enter_context(nc.sbuf_tensor(n, s, d))
    psb = lambda n, s, d: es.enter_context(nc.psum_tensor(n, s, d))

    hbuf = sb("hbuf", [128, KC * 1024], BF16)
    slab = [sb("slabA", [128, KC * OSW], BF16),
            sb("slabB", [128, KC * OSW], BF16)]
    xbuf = [sb("xbufA", [128, D], F32), sb("xbufB", [128, D], F32)]
    tmp = [sb("tmpA", [128, D], BF16), sb("tmpB", [128, D], BF16)]
    r1 = sb("r1", [128, KC * 144], BF16)
    r2 = sb("r2", [128, KC * 128], BF16)
    afin = sb("afin", [128, D], BF16)
    stg = sb("stgA", [128, OSW], BF16)
    btile = sb("btile", [1, 1024], BF16)
    gabt = sb("gabt", [1, 144], BF16)
    idn = sb("idn", [128, 128], BF16)
    onesb = sb("onesb", [1, 128], BF16)
    BTf = sb("BTf", [128, 128], F32)
    nmb = sb("nmb", [128, 4], F32)
    Hp = sb("Hp", [128, 2 * 128], BF16)
    HT = sb("HT", [128, NT * 128], BF16)

    psA = psb("psA", [128, OSW], F32)
    psB = psb("psB", [128, OSW], F32)
    psT = [psb("psT0", [128, 512], BF16), psb("psT1", [128, 512], BF16)]
    psH = [psb("psH0", [128, 144], F32), psb("psH1", [128, 144], F32)]
    aux = psb("aux", [128, 196], F32)
    afbn = psb("afbn", [128, NT * NOS * 6], F32)

    class _View:
        __slots__ = ("t", "off")

        def __init__(self, t, off):
            self.t = t
            self.off = off

        def __getitem__(self, idx):
            rows, cols = idx
            return self.t[rows, cols.start + self.off:cols.stop + self.off]

    stat8 = _View(aux, 0)
    agg8 = _View(aux, 96)
    sgl8 = _View(aux, 100)
    amu = _View(aux, 124)
    arst = _View(aux, 132)
    fmu = amu
    frst = arst
    smx = _View(aux, 140)
    psmm = [psA, psB]

    LGC = Sem(es.enter_context(nc.semaphore("LGC")))
    LGM = [Sem(es.enter_context(nc.semaphore(f"LGM{i}"))) for i in range(8)]
    LGW = [Sem(es.enter_context(nc.semaphore("LGW0"))),
           Sem(es.enter_context(nc.semaphore("LGW1")))]
    LSX = [Sem(es.enter_context(nc.semaphore("LSX0"))),
           Sem(es.enter_context(nc.semaphore("LSX1")))]
    LSA = Sem(es.enter_context(nc.semaphore("LSA")))
    LSF = [Sem(es.enter_context(nc.semaphore("LSF0"))),
           Sem(es.enter_context(nc.semaphore("LSF1")))]
    CSS = [Sem(es.enter_context(nc.semaphore("CSS0"))),
           Sem(es.enter_context(nc.semaphore("CSS1")))]
    CSX = [Sem(es.enter_context(nc.semaphore("CSX0"))),
           Sem(es.enter_context(nc.semaphore("CSX1")))]
    CSF = [Sem(es.enter_context(nc.semaphore(f"CSF{i}"))) for i in range(NT)]
    CCS = Sem(es.enter_context(nc.semaphore("CCS")))
    CCD = Sem(es.enter_context(nc.semaphore("CCD")))
    ONE = Sem(es.enter_context(nc.semaphore("ONE")))
    TG = Sem(es.enter_context(nc.semaphore("TG")))
    TT = Sem(es.enter_context(nc.semaphore("TT")))
    VE = Sem(es.enter_context(nc.semaphore("VE")))
    AC = Sem(es.enter_context(nc.semaphore("AC")))
    PO = Sem(es.enter_context(nc.semaphore("PO")))

    block = es.enter_context(nc.Block())

    def T(f): block.tensor(f)
    def V(f): block.vector(f)
    def A(f): block.scalar(f)
    def G(f): block.gpsimd(f)
    def S(f): block.sync(f)

    def bump(sem, instr, k=1):
        instr.then_inc(sem.h, k)
        sem.n += k
        return sem.n

    def pair_halve(eng, src_t, src_off, dst_ap, w, row_span):
        in0 = bass.AP(src_t, src_off, [[row_span, 128], [2, w]])
        in1 = bass.AP(src_t, src_off + 1, [[row_span, 128], [2, w]])
        return eng.tensor_tensor(dst_ap, in0, in1, ALU.add)

    tb = [0]
    ac_evac = {}
    tmp_guard = {0: [], 1: []}

    def transpose_batch(srcs, writer, extra_tensor_waits=(), evac_engine=None):
        b = tb[0]
        par = b % 2
        prev = ac_evac.get(b - 2)

        @T
        def _(t, b=b, par=par, prev=prev, srcs=srcs, xw=tuple(extra_tensor_waits)):
            for (sh, sv) in xw:
                t.wait_ge(sh, sv)
            if prev:
                t.wait_ge(prev[0], prev[1])
            mm = None
            for j, sap in enumerate(srcs):
                mm = t.transpose(psT[par][:, j * 128:(j + 1) * 128], sap, idn[:, :])
            bump(TT, mm)
        ttv = TT.n

        use_v = (evac_engine == "V")
        if use_v:
            @V
            def _(v, par=par, ttv=ttv, writer=writer):
                v.wait_ge(TT.h, ttv)
                i = writer(v, psT[par])
                bump(VE, i)
            ac_evac[b] = (VE.h, VE.n)
            ev = (VE.h, VE.n)
        else:
            @A
            def _(a, par=par, ttv=ttv, writer=writer):
                a.wait_ge(TT.h, ttv)
                i = writer(a, psT[par])
                bump(AC, i)
            ac_evac[b] = (AC.h, AC.n)
            ev = (AC.h, AC.n)
        tb[0] += 1
        return ttv, ev

    with es:
        @A
        def _(a):
            a.add_instruction(mybir.InstLoadActFuncSet(
                name=a.bass.get_next_instruction_name(),
                act_func_set_id=6, ins=[], outs=[]))

        @G
        def _(g):
            g.dma_start(idn[:, :], idnp[:, :]).then_inc(LGC.h, 16)
            g.dma_start(r2[:, :], selp[:, :]).then_inc(LGC.h, 16)
            g.dma_start(gabt[:, :], gab[:, :]).then_inc(LGC.h, 16)
            g.memset(onesb[:, :], 1.0).then_inc(ONE.h, 1)
        LGC.n += 48
        ONE.n += 1
        p0_lg = LGC.n

        ve_xfree = {}
        ac_rstd = {}
        ac_h_done = {}

        def emit_x_tile(tt):
            xb = xbuf[tt % 2]
            lsx = LSX[tt % 2]
            tp = tmp[tt % 2]

            @S
            def _(s, tt=tt, xb=xb, lsx=lsx):
                if tt >= 2:
                    s.wait_ge(VE.h, ve_xfree[tt - 2])
                s.dma_start(xb[:, :], xin[tt * 128:(tt + 1) * 128, :]).then_inc(lsx.h, 16)
            lsx.n += 16
            ls_x = lsx.n

            @V
            def _(v, tt=tt, xb=xb, lsx=lsx, ls_x=ls_x):
                v.wait_ge(lsx.h, ls_x)
                sb0 = (tt % 2) * 48
                for c in range(8):
                    i = v.bn_stats(stat8[:, sb0 + c * 6:sb0 + (c + 1) * 6],
                                   xb[:, c * 512:(c + 1) * 512])
                    bump(VE, i)
                v.wait_ge(VE.h, VE.n)
                i = v.bn_aggr(agg8[:, (tt % 2) * 2:(tt % 2) * 2 + 2], stat8[:, sb0:sb0 + 48])
                bump(VE, i)
                v.wait_ge(VE.h, VE.n)
                i = v.tensor_scalar(sgl8[:, tt * 3:tt * 3 + 1],
                                    agg8[:, (tt % 2) * 2 + 1:(tt % 2) * 2 + 2], EPS, None, ALU.add)
                bump(VE, i)
            ve_var = VE.n

            @A
            def _(a, tt=tt, ve_var=ve_var):
                a.wait_ge(VE.h, ve_var)
                i = a.activation(sgl8[:, tt * 3 + 1:tt * 3 + 2],
                                 sgl8[:, tt * 3:tt * 3 + 1], AF.Ln, scale=1.0)
                bump(AC, i)
                a.wait_ge(AC.h, AC.n)
                i = a.activation(sgl8[:, tt * 3 + 2:tt * 3 + 3],
                                 sgl8[:, tt * 3 + 1:tt * 3 + 2], AF.Exp, scale=-0.5)
                bump(AC, i)
            ac_rstd[tt] = AC.n

            @V
            def _(v, tt=tt, xb=xb, tp=tp, guards=tuple(tmp_guard[tt % 2])):
                v.wait_ge(AC.h, ac_rstd[tt])
                for (sh, sv) in guards:
                    v.wait_ge(sh, sv)
                i = v.tensor_scalar(tp[:, :], xb[:, :], agg8[:, (tt % 2) * 2:(tt % 2) * 2 + 1],
                                    sgl8[:, tt * 3 + 2:tt * 3 + 3],
                                    ALU.subtract, ALU.mult)
                bump(VE, i)
            ve_xfree[tt] = VE.n
            ve_h1 = VE.n

            for b4 in range(8):
                kc0 = b4 * 4
                srcs = [tp[:, (kc0 + j) * 128:(kc0 + j + 1) * 128] for j in range(4)]

                def writer(a, ps, tt=tt, kc0=kc0):
                    out_ap = bass.AP(hbuf, kc0 * 1024 + tt * 128,
                                     [[hbuf.shape[1], 128], [1024, 4], [1, 128]])
                    in_ap = bass.AP(ps, 0, [[ps.shape[1], 128], [128, 4], [1, 128]])
                    if hasattr(a, "copy"):
                        return a.copy(out_ap, in_ap)
                    return a.tensor_copy(out_ap, in_ap)

                transpose_batch(srcs, writer, [(VE.h, ve_h1)],
                                evac_engine=("V" if b4 % 2 else None))
            tmp_guard[tt % 2] = [(TT.h, TT.n)]
            ac_h_done[tt] = [(AC.h, AC.n), (VE.h, VE.n)]

        tree_done = {}

        def emit_mask_tile(ot):
            mbuf = slab[ot % 2]
            mt_off = ((ot // 2) % 4) * D
            mt = mbuf[:, mt_off:mt_off + D]
            lgm = LGM[ot % 8]

            @G
            def _(g, ot=ot, mt=mt, lgm=lgm):
                if ot >= 8:
                    s_, sv = tree_done[ot - 8]
                    g.wait_ge(s_.h, sv)
                g.dma_start(mt, maskN[ot * 128:(ot + 1) * 128, :]).then_inc(lgm.h, 16)
            lgm.n += 16
            lg_tile = lgm.n

            par = ot % 2
            tp = tmp[par]
            SEM = VE if par == 0 else PO
            EDEC = V if par == 0 else G

            @EDEC
            def _(e, ot=ot, mbuf=mbuf, mt_off=mt_off, lgm=lgm, lg_tile=lg_tile,
                  tp=tp, SEM=SEM, guards=tuple(tmp_guard[par])):
                e.wait_ge(lgm.h, lg_tile)
                if ot >= 2:
                    s_, sv = tree_done[ot - 2]
                    e.wait_ge(s_.h, sv)
                for (sh, sv) in guards:
                    e.wait_ge(sh, sv)
                segs = [(0, 2048), (2048, 1024), (3072, 512), (3584, 256), (3840, 128)]
                src_t, src_off, row_span = mbuf, mt_off, mbuf.shape[1]
                for (o0, w) in segs:
                    last = pair_halve(e, src_t, src_off, tp[:, o0:o0 + w], w, row_span)
                    bump(SEM, last)
                    e.wait_ge(SEM.h, SEM.n)
                    src_t, src_off, row_span = tp, o0, tp.shape[1]
            s1 = SEM.n

            def writer(a, ps, tp=tp):
                return a.copy(tp[:, 128:256], ps[:, 0:128])

            _, ac_ev = transpose_batch([tp[:, 3840:3968]], writer,
                                       [(SEM.h, s1), (LGC.h, p0_lg)])

            @EDEC
            def _(e, ot=ot, tp=tp, SEM=SEM, ac_ev=ac_ev):
                e.wait_ge(ac_ev[0], ac_ev[1])
                segs = [(256, 64), (320, 32), (384, 16), (400, 8), (408, 4)]
                src_t, src_off = tp, 128
                for (o0, w) in segs:
                    last = pair_halve(e, src_t, src_off, tp[:, o0:o0 + w], w, tp.shape[1])
                    bump(SEM, last)
                    e.wait_ge(SEM.h, SEM.n)
                    src_t, src_off = tp, o0
                c = e.tensor_copy(BTf[:, 4 * ot:4 * ot + 4], tp[:, 408:412])
                bump(SEM, c)
            tree_done[ot] = (SEM, SEM.n)
            tmp_guard[par] = [(SEM.h, SEM.n)]

        for ot in range(4):
            emit_mask_tile(ot)
        emit_x_tile(0)

        @A
        def _(a):
            for ot in range(4):
                s_, sv = tree_done[ot]
                a.wait_ge(s_.h, sv)
            a.dma_start(btin[:, :], BTf[:, 0:16]).then_inc(CCS.h, 16)
        CCS.n += 16

        @G
        def _(g):
            g.wait_ge(CCS.h, 16)
            g.collective_compute(
                "AllGather", ALU.bypass,
                replica_groups=[list(range(NCORE))],
                ins=[btin.ap().opt()],
                outs=[btout.ap().opt()],
            ).then_inc(CCD.h, 1)
            g.wait_ge(CCD.h, 1)
            src_ap = bass.AP(btout, 0, [[16, 128], [128 * 16, 8], [1, 16]])
            dst_ap = bass.AP(BTf, 0, [[128, 128], [16, 8], [1, 16]])
            g.dma_start(dst_ap, src_ap).then_inc(LGC.h, 16)
        CCD.n += 1
        LGC.n += 16
        lg_bt = LGC.n

        for _tt in range(1, 6):
            emit_x_tile(_tt)

        @V
        def _(v):
            v.wait_ge(LGC.h, lg_bt)
            c = v.tensor_scalar(Hp[:, 0:128], BTf[:, :], 0.0, 1.0, ALU.max, ALU.min)
            bump(VE, c)
        ve_btb = VE.n

        ve_pb = {}
        for kc in range(KC):
            pss = psH[kc % 2]

            @T
            def _(t, kc=kc, pss=pss):
                t.wait_ge(VE.h, ve_btb)
                if kc >= 2:
                    t.wait_ge(VE.h, ve_pb[kc - 2])
                mm = t.matmul(pss[:, 0:128], r2[:, kc * 128:(kc + 1) * 128],
                              Hp[:, 0:128], start=True, stop=True)
                bump(TT, mm)
            tt_pb = TT.n

            @V
            def _(v, kc=kc, tt_pb=tt_pb, pss=pss):
                v.wait_ge(TT.h, tt_pb)
                c = v.tensor_copy(r1[:, kc * 128:(kc + 1) * 128], pss[:, 0:128])
                bump(VE, c)
            ve_pb[kc] = VE.n
        ve_pbexp_done = VE.n
        tt_sel_end = TT.n

        def wslab_dma(g, dram, os_, sub, dst):
            kc0 = sub * 8
            src = bass.AP(dram, kc0 * 128 * D + os_ * OSW,
                          [[D, 128], [128 * D, 8], [1, OSW]])
            dstv = bass.AP(dst, kc0 * OSW,
                           [[dst.shape[1], 128], [OSW, 8], [1, OSW]])
            return g.dma_start(dstv, src)

        tg_p3_0 = TG.n
        lg_slab = {}
        cs_stg = {}
        ve_ev = {}
        po_mask = {}
        gidx = [0]

        def emit_slab_load(key, dram, os_, guard):
            lgw = LGW[os_ % 2]
            bias_src = battn if key == "m" else bbase

            @G
            def _(g, os_=os_, lgw=lgw, guard=guard, bias_src=bias_src, dram=dram):
                guard(g)
                sl = slab[os_ % 2]
                for sub in range(4):
                    wslab_dma(g, dram, os_, sub, sl).then_inc(lgw.h, 16)
                g.dma_start(btile[0:1, (os_ % 2) * 512:(os_ % 2) * 512 + 512],
                            bias_src[0:1, os_ * OSW:(os_ + 1) * OSW]).then_inc(lgw.h, 16)
            lgw.n += 80
            lg_slab[(key, os_)] = (lgw, lgw.n)

        mask_done = {}

        def _mask_sub_aps(sl, os_, sub):
            kc0 = sub * 8
            out_ap = bass.AP(sl, kc0 * OSW,
                             [[sl.shape[1], 128], [OSW, 8], [32, 16], [1, 32]])
            in1 = bass.AP(r1, kc0 * 128 + os_ * 16,
                          [[r1.shape[1], 128], [128, 8], [1, 16], [0, 32]])
            return out_ap, in1

        def emit_mask_mult(os_, split=False):
            lgw, lv = lg_slab[("m", os_)]
            sl = slab[os_ % 2]
            subs_pool = range(2, 4) if split else range(4)

            if split:
                @V
                def _(v, os_=os_, lgw=lgw, lv=lv, sl=sl):
                    v.wait_ge(lgw.h, lv)
                    v.wait_ge(VE.h, ve_pbexp_done)
                    for sub in range(0, 2):
                        out_ap, in1 = _mask_sub_aps(sl, os_, sub)
                        i = v.tensor_tensor(out_ap, out_ap, in1, ALU.mult)
                        bump(VE, i)
                ve_m = VE.n

            @G
            def _(g, os_=os_, lgw=lgw, lv=lv, sl=sl, subs_pool=subs_pool):
                g.wait_ge(lgw.h, lv)
                g.wait_ge(VE.h, ve_pbexp_done)
                for sub in subs_pool:
                    out_ap, in1 = _mask_sub_aps(sl, os_, sub)
                    i = g.tensor_tensor(out_ap, out_ap, in1, ALU.mult)
                    bump(PO, i)
            po_mask[os_] = PO.n
            mask_done[os_] = [(PO.h, PO.n)] + ([(VE.h, ve_m)] if split else [])

        tg_os_done = {}

        def emit_group(key, os_, tt, extra_mm, stats_fin, dram_out, h_ready,
                       defer=None, stg_slot=None, pre_ve=None):
            g_ = gidx[0]
            ps = psmm[g_ % 2]
            sl = slab[os_ % 2]
            css = CSF[tt] if os_ == NOS - 1 else CSS[0]
            st = stg if stg_slot is None else stg_slot

            @T
            def _(t, os_=os_, tt=tt, sl=sl, ps=ps, g_=g_, key=key):
                if key == "m":
                    for (sh, sv) in mask_done[os_]:
                        t.wait_ge(sh, sv)
                else:
                    lgw, lv = lg_slab[("b", os_)]
                    t.wait_ge(lgw.h, lv)
                    t.wait_ge(LGC.h, lg_ffn)
                for (sh, sv) in h_ready[tt]:
                    t.wait_ge(sh, sv)
                t.wait_ge(ONE.h, 1)
                t.wait_ge(LGC.h, p0_lg)
                if g_ >= 2:
                    t.wait_ge(VE.h, ve_ev[g_ - 2])
                for kc in range(KC):
                    t.matmul(ps[:, :],
                             hbuf[:, kc * 1024 + tt * 128:kc * 1024 + (tt + 1) * 128],
                             sl[:, kc * OSW:(kc + 1) * OSW],
                             start=(kc == 0), stop=False)
                if extra_mm is not None:
                    extra_mm(t, ps)
                mm = t.matmul(ps[:, :], onesb[:, :],
                              btile[0:1, (os_ % 2) * 512:(os_ % 2) * 512 + 512],
                              start=False, stop=True)
                bump(TG, mm)
            tg_g = TG.n

            def emit_evac(os_=os_, tt=tt, ps=ps, st=st, g_=g_, tg_g=tg_g, css=css,
                          dram_out=dram_out, stg_slot=stg_slot, pre_ve=pre_ve):
                @V
                def _(v):
                    v.wait_ge(TG.h, tg_g)
                    if g_ >= 1:
                        ps_, pv = cs_stg[g_ - 1]
                        v.wait_ge(ps_.h, pv)
                    if pre_ve is not None:
                        v.wait_ge(VE.h, pre_ve)
                    i = v.tensor_copy(st[:, :] if stg_slot is None else st, ps[:, :])
                    bump(VE, i)
                    i = v.bn_stats(afbn[:, (tt * NOS + os_) * 6:(tt * NOS + os_ + 1) * 6],
                                   ps[:, :])
                    bump(VE, i)
                ve_ev[g_] = VE.n

                if stg_slot is None:
                    @A
                    def _(a):
                        a.wait_ge(VE.h, ve_ev[g_])
                        a.dma_start(dram_out[tt * 128:(tt + 1) * 128,
                                             os_ * OSW:(os_ + 1) * OSW],
                                    st[:, :]).then_inc(css.h, 16)
                    css.n += 16
                    cs_stg[g_] = (css, css.n)
                else:
                    cs_stg[g_] = (VE, VE.n)

            gidx[0] += 1
            if tt == NT - 1:
                tg_os_done[(key, os_)] = TG.n
            if defer is not None:
                defer.append(emit_evac)
            else:
                emit_evac()
            if os_ == NOS - 1 and stats_fin is not None:
                stats_fin(tt)

        def stats_fin_maker(mu, rst, ac_done):
            def fin(tt):
                @V
                def _(v, tt=tt):
                    v.wait_ge(VE.h, VE.n)
                    i = v.bn_aggr(agg8[:, (tt % 2) * 2:(tt % 2) * 2 + 2],
                                  afbn[:, tt * NOS * 6:(tt + 1) * NOS * 6])
                    bump(VE, i)
                    v.wait_ge(VE.h, VE.n)
                    i = v.tensor_copy(mu[:, tt:tt + 1], agg8[:, (tt % 2) * 2:(tt % 2) * 2 + 1])
                    bump(VE, i)
                    v.wait_ge(VE.h, VE.n)
                    i = v.tensor_scalar(sgl8[:, tt * 3:tt * 3 + 1],
                                        agg8[:, (tt % 2) * 2 + 1:(tt % 2) * 2 + 2], EPS, None, ALU.add)
                    bump(VE, i)
                ve_v = VE.n

                @A
                def _(a, tt=tt, ve_v=ve_v):
                    a.wait_ge(VE.h, ve_v)
                    i = a.activation(sgl8[:, tt * 3 + 1:tt * 3 + 2],
                                     sgl8[:, tt * 3:tt * 3 + 1], AF.Ln, scale=1.0)
                    bump(AC, i)
                    a.wait_ge(AC.h, AC.n)
                    i = a.activation(rst[:, tt:tt + 1], sgl8[:, tt * 3 + 1:tt * 3 + 2],
                                     AF.Exp, scale=-0.5)
                    bump(AC, i)
                ac_done[tt] = AC.n
            return fin

        def p3_guard(os_):
            def guard(g, os_=os_):
                if os_ <= 1:
                    s_, sv = tree_done[2 + os_]
                    g.wait_ge(s_.h, sv)
                else:
                    g.wait_ge(TG.h, tg_os_done[("m", os_ - 2)])
            return guard

        ac_astat = {}
        a_fin = stats_fin_maker(amu, arst, ac_astat)
        emit_slab_load("m", wmT, 0, p3_guard(0))
        emit_mask_mult(0, split=True)
        emit_slab_load("m", wmT, 1, p3_guard(1))
        emit_mask_mult(1, split=True)
        pend = []
        for tt in range(NT):
            if tt + 6 < NT:
                emit_x_tile(tt + 6)
            for e in pend:
                e()
            pend = []
            emit_group("m", 0, tt, None, a_fin, a_dram, ac_h_done, defer=pend)
            emit_group("m", 1, tt, None, a_fin, a_dram, ac_h_done, defer=pend)
        for e in pend:
            e()
        pend = []
        for os_ in range(2, NOS):
            emit_slab_load("m", wmT, os_, p3_guard(os_))
            emit_mask_mult(os_)
            for tt in range(NT):
                emit_group("m", os_, tt, None, a_fin, a_dram, ac_h_done)
        tg_p3_end = TG.n
        css_p3 = (CSS[0].n, CSS[1].n)
        csf_p3 = [CSF[i].n for i in range(NT)]

        @G
        def _(g):
            g.wait_ge(TG.h, tg_p3_end)
            g.wait_ge(TT.h, tt_sel_end)
            g.dma_start(r1[:, :], ga[:, :]).then_inc(LGC.h, 16)
            g.dma_start(r2[:, :], bcat[:, :]).then_inc(LGC.h, 16)
        LGC.n += 32
        lg_ffn = LGC.n

        def p6_guard(os_):
            def guard(g, os_=os_):
                if os_ <= 1:
                    g.wait_ge(TG.h, tg_p3_end)
                else:
                    g.wait_ge(TG.h, tg_os_done[("b", os_ - 2)])
            return guard

        emit_slab_load("b", wbT, 0, p6_guard(0))
        emit_slab_load("b", wbT, 1, p6_guard(1))

        ve_lna = {}
        ve_h2ap = {}
        cs_x2 = {}
        ve_smx_done = {}
        ac_h2_done = {}
        po_x2 = {}
        p5_state = {}

        def p5_stage_a(tt):
            xb = xbuf[tt % 2]
            lsx = LSX[tt % 2]
            tp = tmp[tt % 2]

            @S
            def _(s, tt=tt, xb=xb, lsx=lsx):
                if tt >= 2:
                    cx = cs_x2[tt - 2]
                    s.wait_ge(cx[0].h, cx[1])
                    s.wait_ge(VE.h, ve_h2ap[tt - 2])
                else:
                    s.wait_ge(VE.h, ve_xfree[NT - 2 + tt])
                s.dma_start(xb[:, :], xin[tt * 128:(tt + 1) * 128, :]).then_inc(lsx.h, 16)
                if tt >= 1:
                    s.wait_ge(VE.h, ve_lna[tt - 1])
                s.wait_ge(CSS[0].h, css_p3[0])
                s.wait_ge(CSF[tt].h, csf_p3[tt])
                s.dma_start(afin[:, :], a_dram[tt * 128:(tt + 1) * 128, :]).then_inc(LSA.h, 16)
            lsx.n += 16
            LSA.n += 16
            ls_x, ls_a = lsx.n, LSA.n

            @V
            def _(v, tt=tt, tp=tp, ls_a=ls_a, guards=tuple(tmp_guard[tt % 2])):
                v.wait_ge(LSA.h, ls_a)
                v.wait_ge(AC.h, ac_astat[tt])
                for (sh, sv) in guards:
                    v.wait_ge(sh, sv)
                i = v.tensor_scalar(tp[:, :], afin[:, :], amu[:, tt:tt + 1],
                                    arst[:, tt:tt + 1], ALU.subtract, ALU.mult)
                bump(VE, i)
            ve_lna[tt] = VE.n

            @G
            def _(g, tt=tt, xb=xb, tp=tp, lsx=lsx, ls_x=ls_x):
                g.wait_ge(lsx.h, ls_x)
                g.wait_ge(VE.h, ve_lna[tt])
                i = g.tensor_tensor(xb[:, 0:768], xb[:, 0:768], tp[:, 0:768], ALU.add)
                bump(PO, i)
            po_x2[tt] = PO.n

            @V
            def _(v, tt=tt, xb=xb, tp=tp, lsx=lsx, ls_x=ls_x):
                v.wait_ge(lsx.h, ls_x)
                v.wait_ge(VE.h, ve_lna[tt])
                i = v.tensor_tensor(xb[:, 768:4096], xb[:, 768:4096],
                                    tp[:, 768:4096], ALU.add)
                bump(VE, i)
            ve_x2b = VE.n

            csx = CSX[tt % 2]

            @A
            def _(a, tt=tt, xb=xb, csx=csx, ve_x2b=ve_x2b):
                a.wait_ge(PO.h, po_x2[tt])
                a.wait_ge(VE.h, ve_x2b)
                a.dma_start(x2d[tt * 128:(tt + 1) * 128, :], xb[:, :]).then_inc(csx.h, 16)
            csx.n += 16
            cs_x2[tt] = (csx, csx.n)

            @V
            def _(v, tt=tt, xb=xb, ve_x2b=ve_x2b):
                v.wait_ge(PO.h, po_x2[tt])
                v.wait_ge(VE.h, ve_x2b)
                sb0 = (tt % 2) * 48
                for c in range(8):
                    i = v.bn_stats(stat8[:, sb0 + c * 6:sb0 + (c + 1) * 6],
                                   xb[:, c * 512:(c + 1) * 512])
                    bump(VE, i)
                v.wait_ge(VE.h, VE.n)
                i = v.bn_aggr(agg8[:, (tt % 2) * 2:(tt % 2) * 2 + 2], stat8[:, sb0:sb0 + 48])
                bump(VE, i)
                v.wait_ge(VE.h, VE.n)
                i = v.tensor_scalar(sgl8[:, tt * 3:tt * 3 + 1],
                                    agg8[:, (tt % 2) * 2 + 1:(tt % 2) * 2 + 2], EPS, None, ALU.add)
                bump(VE, i)
            ve_var2 = VE.n

            @A
            def _(a, tt=tt, ve_var2=ve_var2):
                a.wait_ge(VE.h, ve_var2)
                i = a.activation(sgl8[:, tt * 3 + 1:tt * 3 + 2],
                                 sgl8[:, tt * 3:tt * 3 + 1], AF.Ln, scale=1.0)
                bump(AC, i)
                a.wait_ge(AC.h, AC.n)
                i = a.activation(sgl8[:, tt * 3 + 2:tt * 3 + 3],
                                 sgl8[:, tt * 3 + 1:tt * 3 + 2], AF.Exp, scale=-0.5)
                bump(AC, i)
            ac_r2 = AC.n

            @V
            def _(v, tt=tt, xb=xb, tp=tp, ac_r2=ac_r2):
                v.wait_ge(AC.h, ac_r2)
                v.wait_ge(PO.h, po_x2[tt])
                i = v.tensor_scalar(tp[:, :], xb[:, :], agg8[:, (tt % 2) * 2:(tt % 2) * 2 + 1],
                                    sgl8[:, tt * 3 + 2:tt * 3 + 3],
                                    ALU.subtract, ALU.mult)
                bump(VE, i)
            ve_h2ap[tt] = VE.n
            ve_h2 = VE.n

            for b4 in range(8):
                kc0 = b4 * 4
                srcs = [tp[:, (kc0 + j) * 128:(kc0 + j + 1) * 128] for j in range(4)]

                def writer(a, ps, tt=tt, kc0=kc0):
                    out_ap = bass.AP(hbuf, kc0 * 1024 + tt * 128,
                                     [[hbuf.shape[1], 128], [1024, 4], [1, 128]])
                    in_ap = bass.AP(ps, 0, [[ps.shape[1], 128], [128, 4], [1, 128]])
                    if hasattr(a, "copy"):
                        return a.copy(out_ap, in_ap)
                    return a.tensor_copy(out_ap, in_ap)

                transpose_batch(srcs, writer, [(VE.h, ve_h2)],
                                evac_engine=("V" if b4 % 2 else None))
            tmp_guard[tt % 2] = [(TT.h, TT.n)]
            ac_h2_done[tt] = [(AC.h, AC.n), (VE.h, VE.n)]

        def p5_stage_b(tt):
            sp = (tt % 2) * 28
            hp = Hp[:, (tt % 2) * 128:(tt % 2) * 128 + 128]
            psh = psH[tt % 2]

            @T
            def _(t, tt=tt, psh=psh):
                for (sh, sv) in ac_h2_done[tt]:
                    t.wait_ge(sh, sv)
                t.wait_ge(LGC.h, lg_ffn)
                t.wait_ge(ONE.h, 1)
                if tt >= 2:
                    t.wait_ge(VE.h, ve_smx_done[tt - 2])
                for kc in range(KC):
                    t.matmul(psh[:, :],
                             hbuf[:, kc * 1024 + tt * 128:kc * 1024 + (tt + 1) * 128],
                             r1[:, kc * 144:(kc + 1) * 144],
                             start=(kc == 0), stop=False)
                mm = t.matmul(psh[:, :], onesb[:, :], gabt[0:1, :], start=False, stop=True)
                bump(TG, mm)
            tg_gate = TG.n

            @V
            def _(v, tt=tt, psh=psh, tg_gate=tg_gate):
                v.wait_ge(TG.h, tg_gate)
                i = v.reduce_max(nmb[:, (tt % 2):(tt % 2) + 1], psh[:, 128:136],
                                 axis=AXX, negate=True)
                bump(VE, i)
            ve_nm = VE.n

            @A
            def _(a, tt=tt, sp=sp, psh=psh, ve_nm=ve_nm):
                a.wait_ge(VE.h, ve_nm)
                i = a.activation(smx[:, sp + 1:sp + 9], psh[:, 128:136], AF.Exp,
                                 bias=nmb[:, (tt % 2):(tt % 2) + 1], scale=1.0)
                bump(AC, i)
                a.wait_ge(AC.h, AC.n)
                i = a.activation(smx[:, sp + 9:sp + 10], psh[:, 136:137], AF.Exp, scale=-1.0)
                bump(AC, i)
            ac_exp = AC.n

            @V
            def _(v, tt=tt, sp=sp, hp=hp, psh=psh, ac_exp=ac_exp):
                v.wait_ge(AC.h, ac_exp)
                i = v.reduce_sum(smx[:, sp + 10:sp + 11], smx[:, sp + 1:sp + 9], axis=AXX); bump(VE, i)
                v.wait_ge(VE.h, VE.n)
                i = v.reciprocal(smx[:, sp + 11:sp + 12], smx[:, sp + 10:sp + 11]); bump(VE, i)
                i = v.tensor_scalar(smx[:, sp + 12:sp + 13], smx[:, sp + 9:sp + 10],
                                    1.0, None, ALU.add); bump(VE, i)
                v.wait_ge(VE.h, VE.n)
                i = v.reciprocal(smx[:, sp + 13:sp + 14], smx[:, sp + 12:sp + 13]); bump(VE, i)
                v.wait_ge(VE.h, VE.n)
                i = v.tensor_scalar(smx[:, sp + 14:sp + 15], smx[:, sp + 13:sp + 14],
                                    MAXTHR, None, ALU.mult); bump(VE, i)
                v.wait_ge(VE.h, VE.n)
                i = v.tensor_scalar(smx[:, sp + 16:sp + 24], smx[:, sp + 1:sp + 9],
                                    smx[:, sp + 11:sp + 12], smx[:, sp + 14:sp + 15],
                                    ALU.mult, ALU.subtract); bump(VE, i)
                v.wait_ge(VE.h, VE.n)
                i = v.tensor_scalar(smx[:, sp + 16:sp + 24], smx[:, sp + 16:sp + 24],
                                    0.0, None, ALU.max); bump(VE, i)
                v.wait_ge(VE.h, VE.n)
                i = v.reduce_sum(smx[:, sp + 24:sp + 25], smx[:, sp + 16:sp + 24], axis=AXX); bump(VE, i)
                v.wait_ge(VE.h, VE.n)
                i = v.tensor_scalar(nmb[:, 2 + tt % 2:3 + tt % 2], smx[:, sp + 24:sp + 25],
                                    0.0, None, ALU.is_equal); bump(VE, i)
                v.wait_ge(VE.h, VE.n)
                i = v.tensor_tensor(smx[:, sp + 26:sp + 27], smx[:, sp + 24:sp + 25],
                                    nmb[:, 2 + tt % 2:3 + tt % 2], ALU.add); bump(VE, i)
                v.wait_ge(VE.h, VE.n)
                i = v.reciprocal(smx[:, sp + 27:sp + 28], smx[:, sp + 26:sp + 27]); bump(VE, i)
                v.wait_ge(VE.h, VE.n)
                i = v.tensor_scalar(smx[:, sp + 16:sp + 24], smx[:, sp + 16:sp + 24],
                                    smx[:, sp + 27:sp + 28], SCALING,
                                    ALU.mult, ALU.mult); bump(VE, i)
                v.wait_ge(VE.h, VE.n)
                for e in range(8):
                    i = v.tensor_scalar(hp[:, e * 16:(e + 1) * 16], psh[:, e * 16:(e + 1) * 16],
                                        smx[:, sp + 16 + e:sp + 17 + e], None, ALU.mult)
                    bump(VE, i)
            ve_hp = VE.n

            def writerH(a, ps, tt=tt):
                return a.copy(HT[:, tt * 128:(tt + 1) * 128], ps[:, 0:128])

            transpose_batch([hp], writerH, [(VE.h, ve_hp)])

            @V
            def _(v):
                i = v.engine_nop()
                bump(VE, i)
            ve_smx_done[tt] = VE.n
            ac_h2_done[tt] = [(AC.h, AC.n), (VE.h, VE.n)]

            def moe0(t, ps, tt=tt):
                t.matmul(ps[:, :], HT[:, tt * 128:(tt + 1) * 128],
                         r2[:, 0:OSW], start=False, stop=False)
            emit_group("b", 0, tt, moe0, None, f_dram, ac_h2_done, defer=p5_pend)

            def moe1(t, ps, tt=tt):
                t.matmul(ps[:, :], HT[:, tt * 128:(tt + 1) * 128],
                         r2[:, OSW:2 * OSW], start=False, stop=False)
            emit_group("b", 1, tt, moe1, None, f_dram, ac_h2_done, defer=p5_pend)

        p5_pend = []
        for tt in range(NT + 1):
            for e in list(p5_pend):
                e()
            p5_pend.clear()
            if tt >= 1:
                p5_stage_b(tt - 1)
            if tt < NT:
                p5_stage_a(tt)
        for e in list(p5_pend):
            e()
        p5_pend.clear()

        csx_p5 = (CSX[0].n, CSX[1].n)

        ac_fstat = {}
        f_fin = stats_fin_maker(fmu, frst, ac_fstat)
        ve_lnf = {}
        cs_out = {}
        po_out = {}

        def emit_p8_tile(tt):
            xb = xbuf[tt % 2]
            lsx = LSX[tt % 2]
            tp = tmp[tt % 2]
            css0_v = CSS[0].n
            lsf = LSF[tt % 2]
            W7 = (NOS - 1) * OSW
            fbuf = slab[0][:, (tt % 2) * D:(tt % 2) * D + W7]

            @S
            def _(s, tt=tt, xb=xb, lsx=lsx, css0_v=css0_v,
                  lsf=lsf, fbuf=fbuf, W7=W7):
                s.wait_ge(CSS[0].h, css0_v)
                s.wait_ge(CSX[0].h, csx_p5[0])
                s.wait_ge(CSX[1].h, csx_p5[1])
                if tt >= 2:
                    co = cs_out[tt - 2]
                    s.wait_ge(co[0].h, co[1])
                s.dma_start(xb[:, :], x2d[tt * 128:(tt + 1) * 128, :]).then_inc(lsx.h, 16)
                s.wait_ge(TG.h, tg_os_done[("b", NOS - 2)])
                if tt >= 2:
                    s.wait_ge(VE.h, ve_lnf[tt - 2])
                s.dma_start(fbuf, f_dram[tt * 128:(tt + 1) * 128, 0:W7]).then_inc(lsf.h, 16)
            lsx.n += 16
            lsf.n += 16
            ls_x, ls_f = lsx.n, lsf.n

            st7, ve7 = stg7[tt]

            @V
            def _(v, tt=tt, tp=tp, ls_f=ls_f, lsf=lsf, fbuf=fbuf, W7=W7,
                  st7=st7, ve7=ve7, guards=tuple(tmp_guard[tt % 2])):
                v.wait_ge(lsf.h, ls_f)
                v.wait_ge(AC.h, ac_fstat[tt])
                v.wait_ge(VE.h, ve7)
                for (sh, sv) in guards:
                    v.wait_ge(sh, sv)
                i = v.tensor_scalar(tp[:, 0:W7], fbuf, fmu[:, tt:tt + 1],
                                    frst[:, tt:tt + 1], ALU.subtract, ALU.mult)
                bump(VE, i)
                i = v.tensor_scalar(tp[:, W7:D], st7, fmu[:, tt:tt + 1],
                                    frst[:, tt:tt + 1], ALU.subtract, ALU.mult)
                bump(VE, i)
            ve_lnf[tt] = VE.n

            @G
            def _(g, tt=tt, xb=xb, tp=tp, lsx=lsx, ls_x=ls_x):
                g.wait_ge(lsx.h, ls_x)
                g.wait_ge(VE.h, ve_lnf[tt])
                i = g.tensor_tensor(xb[:, 0:768], xb[:, 0:768], tp[:, 0:768], ALU.add)
                bump(PO, i)
            po_out[tt] = PO.n

            @V
            def _(v, tt=tt, xb=xb, tp=tp, lsx=lsx, ls_x=ls_x):
                v.wait_ge(lsx.h, ls_x)
                v.wait_ge(VE.h, ve_lnf[tt])
                i = v.tensor_tensor(xb[:, 768:4096], xb[:, 768:4096],
                                    tp[:, 768:4096], ALU.add)
                bump(VE, i)
            ve_ob = VE.n
            tmp_guard[tt % 2] = [(PO.h, PO.n), (VE.h, ve_ob)]

            csx = CSX[tt % 2]

            @A
            def _(a, tt=tt, xb=xb, csx=csx, ve_ob=ve_ob):
                a.wait_ge(PO.h, po_out[tt])
                a.wait_ge(VE.h, ve_ob)
                a.dma_start(outd[tt * 128:(tt + 1) * 128, 0:2048],
                            xb[:, 0:2048]).then_inc(csx.h, 16)

            @S
            def _(s, tt=tt, xb=xb, csx=csx, ve_ob=ve_ob):
                s.wait_ge(PO.h, po_out[tt])
                s.wait_ge(VE.h, ve_ob)
                s.dma_start(outd[tt * 128:(tt + 1) * 128, 2048:4096],
                            xb[:, 2048:4096]).then_inc(csx.h, 16)
            csx.n += 32
            cs_out[tt] = (csx, csx.n)

        stg7 = {}
        for os_ in range(2, NOS):
            emit_slab_load("b", wbT, os_, p6_guard(os_))
            for tt in range(NT):
                def moe(t, ps, os_=os_, tt=tt):
                    t.matmul(ps[:, :], HT[:, tt * 128:(tt + 1) * 128],
                             r2[:, os_ * OSW:(os_ + 1) * OSW], start=False, stop=False)
                if os_ == NOS - 1:
                    slot = slab[0][:, 2 * D + (tt % 2) * OSW:
                                   2 * D + (tt % 2) * OSW + OSW]
                    emit_group("b", os_, tt, moe, None, f_dram, ac_h2_done,
                               stg_slot=slot, pre_ve=ve_lnf.get(tt - 2))
                    stg7[tt] = (slot, VE.n)
                else:
                    emit_group("b", os_, tt, moe, f_fin, f_dram, ac_h2_done)
                if os_ == NOS - 1 and tt >= 1:
                    f_fin(tt - 1)
                    emit_p8_tile(tt - 1)
        f_fin(NT - 1)
        emit_p8_tile(NT - 1)
        css_p6 = (CSS[0].n, CSS[1].n)

        @A
        def _(a):
            a.wait_ge(CSX[0].h, CSX[0].n)
            a.wait_ge(CSX[1].h, CSX[1].n)

        @S
        def _(s):
            s.wait_ge(CSX[0].h, CSX[0].n)
            s.wait_ge(CSX[1].h, CSX[1].n)

    return nc


_NC_CACHE = None


def _host_pack(inputs):
    L0_attn_data = np.asarray(inputs["L0_attn_data"], dtype=np.float32)
    mask = np.ascontiguousarray(np.asarray(inputs["L0_attn_mask"], dtype=np.int32))
    weight = np.asarray(inputs["L0_attn_weight"], dtype=np.float32)
    battn = np.asarray(inputs["L0_attn_bias"], dtype=np.float32)
    W_base = np.asarray(inputs["L0_ffn_W_base"], dtype=np.float32)
    b_base = np.asarray(inputs["L0_ffn_b_base"], dtype=np.float32)
    W_gate = np.asarray(inputs["L0_ffn_W_gate"], dtype=np.float32)
    W_thr = np.asarray(inputs["L0_ffn_W_thr"], dtype=np.float32)
    b_thr = np.asarray(inputs["L0_ffn_b_thr"], dtype=np.float32)
    A_w = np.asarray(inputs["L0_ffn_A_w"], dtype=np.float32)
    B_w = np.asarray(inputs["L0_ffn_B_w"], dtype=np.float32)

    x8 = np.ascontiguousarray(L0_attn_data.reshape(NCORE * TPC, D))
    wmT = np.ascontiguousarray(weight.T)
    wbT = np.ascontiguousarray(W_base.T)

    AcatT = A_w.transpose(2, 0, 1).reshape(D, 128)
    gcols = np.concatenate([W_gate.T, W_thr[:, None]], axis=1)
    gap = np.zeros((128, KC * 144), dtype=np.float32)
    for kc in range(KC):
        gap[:, kc * 144:kc * 144 + 128] = AcatT[kc * 128:(kc + 1) * 128, :]
        gap[:, kc * 144 + 128:kc * 144 + 137] = gcols[kc * 128:(kc + 1) * 128, :]
    gab = np.zeros((1, 144), dtype=np.float32)
    gab[0, 136] = b_thr[0]
    bcat = np.ascontiguousarray(B_w.transpose(0, 2, 1).reshape(128, D))

    kb = np.arange(128)[:, None]
    p = np.arange(128)[None, :]
    selp = np.zeros((128, KC * 128), dtype=BF)
    for kc in range(KC):
        selp[:, kc * 128:(kc + 1) * 128] = (kb == 4 * kc + p // 32).astype(BF)
    idnp = np.eye(128, dtype=BF)

    shared = {
        "wmT": wmT, "wbT": wbT, "ga": gap, "gab": gab,
        "bcat": bcat, "battn": np.ascontiguousarray(battn[None, :]),
        "bbase": np.ascontiguousarray(b_base[None, :]),
        "selp": selp, "idnp": idnp,
    }
    in_maps = []
    for c in range(NCORE):
        m = dict(shared)
        m["xin"] = np.ascontiguousarray(x8[c * TPC:(c + 1) * TPC, :])
        m["maskN"] = np.ascontiguousarray(mask[c * (D // 8):(c + 1) * (D // 8), :])
        in_maps.append(m)
    return in_maps


def kernel(**inputs):
    global _NC_CACHE
    in_maps = _host_pack(inputs)
    if _NC_CACHE is None:
        _NC_CACHE = build_nc()
    res = run_bass_kernel_spmd(_NC_CACHE, in_maps, core_ids=list(range(NCORE)))
    outs = [np.asarray(res.results[c]["outd"]) for c in range(NCORE)]
    full = np.concatenate(outs, axis=0).reshape(4, 2048, D).astype(np.float32)
    return full


# version 27
# speedup vs baseline: 1.0437x; 1.0437x over previous
import numpy as np
import ml_dtypes
import concourse.bass as bass
import concourse.mybir as mybir
from concourse.alu_op_type import AluOpType as ALU
from concourse.bass_utils import run_bass_kernel_spmd
from contextlib import ExitStack

F32 = mybir.dt.float32
BF16 = mybir.dt.bfloat16
I32 = mybir.dt.int32
AF = mybir.ActivationFunctionType
AXX = mybir.AxisListType.X
BF = ml_dtypes.bfloat16

NCORE = 8
D = 4096
TPC = 1024
NT = TPC // 128
KC = D // 128
OSW = 512
NOS = D // OSW
EPS = 1e-5
MAXTHR = 0.125
SCALING = 2.0


class Sem:
    __slots__ = ("h", "n")

    def __init__(self, h):
        self.h = h
        self.n = 0


def build_nc():
    nc = bass.Bass("TRN2", target_bir_lowering=False)

    xin = nc.dram_tensor("xin", [TPC, D], F32, kind="ExternalInput")
    wmT = nc.dram_tensor("wmT", [D, D], F32, kind="ExternalInput")
    wbT = nc.dram_tensor("wbT", [D, D], F32, kind="ExternalInput")
    maskN = nc.dram_tensor("maskN", [D // 8, D], I32, kind="ExternalInput")
    btin = nc.dram_tensor("btin", [128, 16], F32)
    btout = nc.dram_tensor("btout", [1024, 16], F32)
    ga = nc.dram_tensor("ga", [128, KC * 144], F32, kind="ExternalInput")
    gab = nc.dram_tensor("gab", [1, 144], F32, kind="ExternalInput")
    bcat = nc.dram_tensor("bcat", [128, D], F32, kind="ExternalInput")
    battn = nc.dram_tensor("battn", [1, D], F32, kind="ExternalInput")
    bbase = nc.dram_tensor("bbase", [1, D], F32, kind="ExternalInput")
    selp = nc.dram_tensor("selp", [128, KC * 128], BF16, kind="ExternalInput")
    idnp = nc.dram_tensor("idnp", [128, 128], BF16, kind="ExternalInput")
    x2d = nc.dram_tensor("x2d", [TPC, D], F32)
    a_dram = nc.dram_tensor("a_dram", [TPC, D], BF16)
    f_dram = nc.dram_tensor("f_dram", [TPC, D], BF16)
    outd = nc.dram_tensor("outd", [TPC, D], F32, kind="ExternalOutput")

    es = ExitStack()
    sb = lambda n, s, d: es.enter_context(nc.sbuf_tensor(n, s, d))
    psb = lambda n, s, d: es.enter_context(nc.psum_tensor(n, s, d))

    hbuf = sb("hbuf", [128, KC * 1024], BF16)
    slab = [sb("slabA", [128, KC * OSW], BF16),
            sb("slabB", [128, KC * OSW], BF16)]
    xbuf = [sb("xbufA", [128, D], F32), sb("xbufB", [128, D], F32)]
    tmp = [sb("tmpA", [128, D], BF16), sb("tmpB", [128, D], BF16)]
    r1 = sb("r1", [128, KC * 144], BF16)
    r2 = sb("r2", [128, KC * 128], BF16)
    afin = sb("afin", [128, D], BF16)
    stg = sb("stgA", [128, OSW], BF16)
    btile = sb("btile", [1, 1024], BF16)
    gabt = sb("gabt", [1, 144], BF16)
    idn = sb("idn", [128, 128], BF16)
    onesb = sb("onesb", [1, 128], BF16)
    BTf = sb("BTf", [128, 128], F32)
    nmb = sb("nmb", [128, 4], F32)
    Hp = sb("Hp", [128, 2 * 128], BF16)
    HT = sb("HT", [128, NT * 128], BF16)

    psA = psb("psA", [128, OSW], F32)
    psB = psb("psB", [128, OSW], F32)
    psT = [psb("psT0", [128, 512], BF16), psb("psT1", [128, 512], BF16)]
    psH = [psb("psH0", [128, 144], F32), psb("psH1", [128, 144], F32)]
    aux = psb("aux", [128, 196], F32)
    afbn = psb("afbn", [128, NT * NOS * 6], F32)

    class _View:
        __slots__ = ("t", "off")

        def __init__(self, t, off):
            self.t = t
            self.off = off

        def __getitem__(self, idx):
            rows, cols = idx
            return self.t[rows, cols.start + self.off:cols.stop + self.off]

    stat8 = _View(aux, 0)
    agg8 = _View(aux, 96)
    sgl8 = _View(aux, 100)
    amu = _View(aux, 124)
    arst = _View(aux, 132)
    fmu = amu
    frst = arst
    smx = _View(aux, 140)
    psmm = [psA, psB]

    LGC = Sem(es.enter_context(nc.semaphore("LGC")))
    LGM = [Sem(es.enter_context(nc.semaphore(f"LGM{i}"))) for i in range(8)]
    LGW = [Sem(es.enter_context(nc.semaphore("LGW0"))),
           Sem(es.enter_context(nc.semaphore("LGW1")))]
    LSX = [Sem(es.enter_context(nc.semaphore("LSX0"))),
           Sem(es.enter_context(nc.semaphore("LSX1")))]
    LSA = Sem(es.enter_context(nc.semaphore("LSA")))
    LSF = [Sem(es.enter_context(nc.semaphore("LSF0"))),
           Sem(es.enter_context(nc.semaphore("LSF1")))]
    CSS = [Sem(es.enter_context(nc.semaphore("CSS0"))),
           Sem(es.enter_context(nc.semaphore("CSS1")))]
    CSX = [Sem(es.enter_context(nc.semaphore("CSX0"))),
           Sem(es.enter_context(nc.semaphore("CSX1")))]
    CSF = [Sem(es.enter_context(nc.semaphore(f"CSF{i}"))) for i in range(NT)]
    CCS = Sem(es.enter_context(nc.semaphore("CCS")))
    CCD = Sem(es.enter_context(nc.semaphore("CCD")))
    ONE = Sem(es.enter_context(nc.semaphore("ONE")))
    TG = Sem(es.enter_context(nc.semaphore("TG")))
    TT = Sem(es.enter_context(nc.semaphore("TT")))
    VE = Sem(es.enter_context(nc.semaphore("VE")))
    AC = Sem(es.enter_context(nc.semaphore("AC")))
    PO = Sem(es.enter_context(nc.semaphore("PO")))

    block = es.enter_context(nc.Block())

    def T(f): block.tensor(f)
    def V(f): block.vector(f)
    def A(f): block.scalar(f)
    def G(f): block.gpsimd(f)
    def S(f): block.sync(f)

    def bump(sem, instr, k=1):
        instr.then_inc(sem.h, k)
        sem.n += k
        return sem.n

    def pair_halve(eng, src_t, src_off, dst_ap, w, row_span):
        in0 = bass.AP(src_t, src_off, [[row_span, 128], [2, w]])
        in1 = bass.AP(src_t, src_off + 1, [[row_span, 128], [2, w]])
        return eng.tensor_tensor(dst_ap, in0, in1, ALU.add)

    tb = [0]
    ac_evac = {}
    tmp_guard = {0: [], 1: []}

    def transpose_batch(srcs, writer, extra_tensor_waits=(), evac_engine=None):
        b = tb[0]
        par = b % 2
        prev = ac_evac.get(b - 2)

        @T
        def _(t, b=b, par=par, prev=prev, srcs=srcs, xw=tuple(extra_tensor_waits)):
            for (sh, sv) in xw:
                t.wait_ge(sh, sv)
            if prev:
                t.wait_ge(prev[0], prev[1])
            mm = None
            for j, sap in enumerate(srcs):
                mm = t.transpose(psT[par][:, j * 128:(j + 1) * 128], sap, idn[:, :])
            bump(TT, mm)
        ttv = TT.n

        use_v = (evac_engine == "V")
        if use_v:
            @V
            def _(v, par=par, ttv=ttv, writer=writer):
                v.wait_ge(TT.h, ttv)
                i = writer(v, psT[par])
                bump(VE, i)
            ac_evac[b] = (VE.h, VE.n)
            ev = (VE.h, VE.n)
        else:
            @A
            def _(a, par=par, ttv=ttv, writer=writer):
                a.wait_ge(TT.h, ttv)
                i = writer(a, psT[par])
                bump(AC, i)
            ac_evac[b] = (AC.h, AC.n)
            ev = (AC.h, AC.n)
        tb[0] += 1
        return ttv, ev

    with es:
        @A
        def _(a):
            a.add_instruction(mybir.InstLoadActFuncSet(
                name=a.bass.get_next_instruction_name(),
                act_func_set_id=6, ins=[], outs=[]))

        @G
        def _(g):
            g.dma_start(idn[:, :], idnp[:, :]).then_inc(LGC.h, 16)
            g.dma_start(r2[:, :], selp[:, :]).then_inc(LGC.h, 16)
            g.dma_start(gabt[:, :], gab[:, :]).then_inc(LGC.h, 16)
            g.memset(onesb[:, :], 1.0).then_inc(ONE.h, 1)
        LGC.n += 48
        ONE.n += 1
        p0_lg = LGC.n

        ve_xfree = {}
        ac_rstd = {}
        ac_h_done = {}

        def emit_x_tile(tt):
            xb = xbuf[tt % 2]
            lsx = LSX[tt % 2]
            tp = tmp[tt % 2]

            @S
            def _(s, tt=tt, xb=xb, lsx=lsx):
                if tt >= 2:
                    s.wait_ge(VE.h, ve_xfree[tt - 2])
                s.dma_start(xb[:, :], xin[tt * 128:(tt + 1) * 128, :]).then_inc(lsx.h, 16)
            lsx.n += 16
            ls_x = lsx.n

            @V
            def _(v, tt=tt, xb=xb, lsx=lsx, ls_x=ls_x):
                v.wait_ge(lsx.h, ls_x)
                sb0 = (tt % 2) * 48
                for c in range(8):
                    i = v.bn_stats(stat8[:, sb0 + c * 6:sb0 + (c + 1) * 6],
                                   xb[:, c * 512:(c + 1) * 512])
                    bump(VE, i)
                v.wait_ge(VE.h, VE.n)
                i = v.bn_aggr(agg8[:, (tt % 2) * 2:(tt % 2) * 2 + 2], stat8[:, sb0:sb0 + 48])
                bump(VE, i)
                v.wait_ge(VE.h, VE.n)
                i = v.tensor_scalar(sgl8[:, tt * 3:tt * 3 + 1],
                                    agg8[:, (tt % 2) * 2 + 1:(tt % 2) * 2 + 2], EPS, None, ALU.add)
                bump(VE, i)
            ve_var = VE.n

            @A
            def _(a, tt=tt, ve_var=ve_var):
                a.wait_ge(VE.h, ve_var)
                i = a.activation(sgl8[:, tt * 3 + 1:tt * 3 + 2],
                                 sgl8[:, tt * 3:tt * 3 + 1], AF.Ln, scale=1.0)
                bump(AC, i)
                a.wait_ge(AC.h, AC.n)
                i = a.activation(sgl8[:, tt * 3 + 2:tt * 3 + 3],
                                 sgl8[:, tt * 3 + 1:tt * 3 + 2], AF.Exp, scale=-0.5)
                bump(AC, i)
            ac_rstd[tt] = AC.n

            @V
            def _(v, tt=tt, xb=xb, tp=tp, guards=tuple(tmp_guard[tt % 2])):
                v.wait_ge(AC.h, ac_rstd[tt])
                for (sh, sv) in guards:
                    v.wait_ge(sh, sv)
                i = v.tensor_scalar(tp[:, :], xb[:, :], agg8[:, (tt % 2) * 2:(tt % 2) * 2 + 1],
                                    sgl8[:, tt * 3 + 2:tt * 3 + 3],
                                    ALU.subtract, ALU.mult)
                bump(VE, i)
            ve_xfree[tt] = VE.n
            ve_h1 = VE.n

            for b4 in range(8):
                kc0 = b4 * 4
                srcs = [tp[:, (kc0 + j) * 128:(kc0 + j + 1) * 128] for j in range(4)]

                def writer(a, ps, tt=tt, kc0=kc0):
                    out_ap = bass.AP(hbuf, kc0 * 1024 + tt * 128,
                                     [[hbuf.shape[1], 128], [1024, 4], [1, 128]])
                    in_ap = bass.AP(ps, 0, [[ps.shape[1], 128], [128, 4], [1, 128]])
                    if hasattr(a, "copy"):
                        return a.copy(out_ap, in_ap)
                    return a.tensor_copy(out_ap, in_ap)

                transpose_batch(srcs, writer, [(VE.h, ve_h1)],
                                evac_engine=("V" if b4 % 2 else None))
            tmp_guard[tt % 2] = [(TT.h, TT.n)]
            ac_h_done[tt] = [(AC.h, AC.n), (VE.h, VE.n)]

        tree_done = {}

        def emit_mask_tile(ot):
            mbuf = slab[ot % 2]
            mt_off = ((ot // 2) % 4) * D
            mt = mbuf[:, mt_off:mt_off + D]
            lgm = LGM[ot % 8]

            @G
            def _(g, ot=ot, mt=mt, lgm=lgm):
                if ot >= 8:
                    s_, sv = tree_done[ot - 8]
                    g.wait_ge(s_.h, sv)
                g.dma_start(mt, maskN[ot * 128:(ot + 1) * 128, :]).then_inc(lgm.h, 16)
            lgm.n += 16
            lg_tile = lgm.n

            par = ot % 2
            tp = tmp[par]
            SEM = VE if par == 0 else PO
            EDEC = V if par == 0 else G

            @EDEC
            def _(e, ot=ot, mbuf=mbuf, mt_off=mt_off, lgm=lgm, lg_tile=lg_tile,
                  tp=tp, SEM=SEM, guards=tuple(tmp_guard[par])):
                e.wait_ge(lgm.h, lg_tile)
                if ot >= 2:
                    s_, sv = tree_done[ot - 2]
                    e.wait_ge(s_.h, sv)
                for (sh, sv) in guards:
                    e.wait_ge(sh, sv)
                segs = [(0, 2048), (2048, 1024), (3072, 512), (3584, 256), (3840, 128)]
                src_t, src_off, row_span = mbuf, mt_off, mbuf.shape[1]
                for (o0, w) in segs:
                    last = pair_halve(e, src_t, src_off, tp[:, o0:o0 + w], w, row_span)
                    bump(SEM, last)
                    e.wait_ge(SEM.h, SEM.n)
                    src_t, src_off, row_span = tp, o0, tp.shape[1]
            s1 = SEM.n

            def writer(a, ps, tp=tp):
                return a.copy(tp[:, 128:256], ps[:, 0:128])

            _, ac_ev = transpose_batch([tp[:, 3840:3968]], writer,
                                       [(SEM.h, s1), (LGC.h, p0_lg)])

            @EDEC
            def _(e, ot=ot, tp=tp, SEM=SEM, ac_ev=ac_ev):
                e.wait_ge(ac_ev[0], ac_ev[1])
                segs = [(256, 64), (320, 32), (384, 16), (400, 8), (408, 4)]
                src_t, src_off = tp, 128
                for (o0, w) in segs:
                    last = pair_halve(e, src_t, src_off, tp[:, o0:o0 + w], w, tp.shape[1])
                    bump(SEM, last)
                    e.wait_ge(SEM.h, SEM.n)
                    src_t, src_off = tp, o0
                c = e.tensor_copy(BTf[:, 4 * ot:4 * ot + 4], tp[:, 408:412])
                bump(SEM, c)
            tree_done[ot] = (SEM, SEM.n)
            tmp_guard[par] = [(SEM.h, SEM.n)]

        for ot in range(4):
            emit_mask_tile(ot)
        emit_x_tile(0)

        @A
        def _(a):
            for ot in range(4):
                s_, sv = tree_done[ot]
                a.wait_ge(s_.h, sv)
            a.dma_start(btin[:, :], BTf[:, 0:16]).then_inc(CCS.h, 16)
        CCS.n += 16

        @G
        def _(g):
            g.wait_ge(CCS.h, 16)
            g.collective_compute(
                "AllGather", ALU.bypass,
                replica_groups=[list(range(NCORE))],
                ins=[btin.ap().opt()],
                outs=[btout.ap().opt()],
            ).then_inc(CCD.h, 1)
            g.wait_ge(CCD.h, 1)
            src_ap = bass.AP(btout, 0, [[16, 128], [128 * 16, 8], [1, 16]])
            dst_ap = bass.AP(BTf, 0, [[128, 128], [16, 8], [1, 16]])
            g.dma_start(dst_ap, src_ap).then_inc(LGC.h, 16)
        CCD.n += 1
        LGC.n += 16
        lg_bt = LGC.n

        for _tt in range(1, 6):
            emit_x_tile(_tt)

        @V
        def _(v):
            v.wait_ge(LGC.h, lg_bt)
            c = v.tensor_scalar(Hp[:, 0:128], BTf[:, :], 0.0, 1.0, ALU.max, ALU.min)
            bump(VE, c)
        ve_btb = VE.n

        ve_pb = {}
        for kc in range(KC):
            pss = psH[kc % 2]

            @T
            def _(t, kc=kc, pss=pss):
                t.wait_ge(VE.h, ve_btb)
                if kc >= 2:
                    t.wait_ge(VE.h, ve_pb[kc - 2])
                mm = t.matmul(pss[:, 0:128], r2[:, kc * 128:(kc + 1) * 128],
                              Hp[:, 0:128], start=True, stop=True)
                bump(TT, mm)
            tt_pb = TT.n

            @V
            def _(v, kc=kc, tt_pb=tt_pb, pss=pss):
                v.wait_ge(TT.h, tt_pb)
                c = v.tensor_copy(r1[:, kc * 128:(kc + 1) * 128], pss[:, 0:128])
                bump(VE, c)
            ve_pb[kc] = VE.n
        ve_pbexp_done = VE.n
        tt_sel_end = TT.n

        def wslab_dma(g, dram, os_, sub, dst):
            kc0 = sub * 8
            src = bass.AP(dram, kc0 * 128 * D + os_ * OSW,
                          [[D, 128], [128 * D, 8], [1, OSW]])
            dstv = bass.AP(dst, kc0 * OSW,
                           [[dst.shape[1], 128], [OSW, 8], [1, OSW]])
            return g.dma_start(dstv, src)

        tg_p3_0 = TG.n
        lg_slab = {}
        cs_stg = {}
        ve_ev = {}
        po_mask = {}
        gidx = [0]

        def emit_slab_load(key, dram, os_, guard):
            lgw = LGW[os_ % 2]
            bias_src = battn if key == "m" else bbase

            @G
            def _(g, os_=os_, lgw=lgw, guard=guard, bias_src=bias_src, dram=dram):
                guard(g)
                sl = slab[os_ % 2]
                for sub in range(4):
                    wslab_dma(g, dram, os_, sub, sl).then_inc(lgw.h, 16)
                g.dma_start(btile[0:1, (os_ % 2) * 512:(os_ % 2) * 512 + 512],
                            bias_src[0:1, os_ * OSW:(os_ + 1) * OSW]).then_inc(lgw.h, 16)
            lgw.n += 80
            lg_slab[(key, os_)] = (lgw, lgw.n)

        mask_done = {}

        def _mask_sub_aps(sl, os_, sub):
            kc0 = sub * 8
            out_ap = bass.AP(sl, kc0 * OSW,
                             [[sl.shape[1], 128], [OSW, 8], [32, 16], [1, 32]])
            in1 = bass.AP(r1, kc0 * 128 + os_ * 16,
                          [[r1.shape[1], 128], [128, 8], [1, 16], [0, 32]])
            return out_ap, in1

        def emit_mask_mult(os_, split=False):
            lgw, lv = lg_slab[("m", os_)]
            sl = slab[os_ % 2]
            subs_pool = range(2, 4) if split else range(4)

            if split:
                @V
                def _(v, os_=os_, lgw=lgw, lv=lv, sl=sl):
                    v.wait_ge(lgw.h, lv)
                    v.wait_ge(VE.h, ve_pbexp_done)
                    for sub in range(0, 2):
                        out_ap, in1 = _mask_sub_aps(sl, os_, sub)
                        i = v.tensor_tensor(out_ap, out_ap, in1, ALU.mult)
                        bump(VE, i)
                ve_m = VE.n

            @G
            def _(g, os_=os_, lgw=lgw, lv=lv, sl=sl, subs_pool=subs_pool):
                g.wait_ge(lgw.h, lv)
                g.wait_ge(VE.h, ve_pbexp_done)
                for sub in subs_pool:
                    out_ap, in1 = _mask_sub_aps(sl, os_, sub)
                    i = g.tensor_tensor(out_ap, out_ap, in1, ALU.mult)
                    bump(PO, i)
            po_mask[os_] = PO.n
            mask_done[os_] = [(PO.h, PO.n)] + ([(VE.h, ve_m)] if split else [])

        tg_os_done = {}

        def emit_group(key, os_, tt, extra_mm, stats_fin, dram_out, h_ready,
                       defer=None, stg_slot=None, pre_ve=None):
            g_ = gidx[0]
            ps = psmm[g_ % 2]
            sl = slab[os_ % 2]
            css = CSF[tt] if os_ == NOS - 1 else CSS[0]
            st = stg if stg_slot is None else stg_slot

            @T
            def _(t, os_=os_, tt=tt, sl=sl, ps=ps, g_=g_, key=key):
                if key == "m":
                    for (sh, sv) in mask_done[os_]:
                        t.wait_ge(sh, sv)
                else:
                    lgw, lv = lg_slab[("b", os_)]
                    t.wait_ge(lgw.h, lv)
                    t.wait_ge(LGC.h, lg_ffn)
                for (sh, sv) in h_ready[tt]:
                    t.wait_ge(sh, sv)
                t.wait_ge(ONE.h, 1)
                t.wait_ge(LGC.h, p0_lg)
                if g_ >= 2:
                    t.wait_ge(VE.h, ve_ev[g_ - 2])
                for kc in range(KC):
                    t.matmul(ps[:, :],
                             hbuf[:, kc * 1024 + tt * 128:kc * 1024 + (tt + 1) * 128],
                             sl[:, kc * OSW:(kc + 1) * OSW],
                             start=(kc == 0), stop=False)
                if extra_mm is not None:
                    extra_mm(t, ps)
                mm = t.matmul(ps[:, :], onesb[:, :],
                              btile[0:1, (os_ % 2) * 512:(os_ % 2) * 512 + 512],
                              start=False, stop=True)
                bump(TG, mm)
            tg_g = TG.n

            def emit_evac(os_=os_, tt=tt, ps=ps, st=st, g_=g_, tg_g=tg_g, css=css,
                          dram_out=dram_out, stg_slot=stg_slot, pre_ve=pre_ve):
                @V
                def _(v):
                    v.wait_ge(TG.h, tg_g)
                    if g_ >= 1:
                        ps_, pv = cs_stg[g_ - 1]
                        v.wait_ge(ps_.h, pv)
                    if pre_ve is not None:
                        v.wait_ge(VE.h, pre_ve)
                    i = v.tensor_copy(st[:, :] if stg_slot is None else st, ps[:, :])
                    bump(VE, i)
                    i = v.bn_stats(afbn[:, (tt * NOS + os_) * 6:(tt * NOS + os_ + 1) * 6],
                                   ps[:, :])
                    bump(VE, i)
                ve_ev[g_] = VE.n

                if stg_slot is None:
                    @A
                    def _(a):
                        a.wait_ge(VE.h, ve_ev[g_])
                        a.dma_start(dram_out[tt * 128:(tt + 1) * 128,
                                             os_ * OSW:(os_ + 1) * OSW],
                                    st[:, :]).then_inc(css.h, 16)
                    css.n += 16
                    cs_stg[g_] = (css, css.n)
                else:
                    cs_stg[g_] = (VE, VE.n)

            gidx[0] += 1
            if tt == NT - 1:
                tg_os_done[(key, os_)] = TG.n
            if defer is not None:
                defer.append(emit_evac)
            else:
                emit_evac()
            if os_ == NOS - 1 and stats_fin is not None:
                stats_fin(tt)

        def stats_fin_maker(mu, rst, ac_done):
            def fin(tt):
                @V
                def _(v, tt=tt):
                    v.wait_ge(VE.h, VE.n)
                    i = v.bn_aggr(agg8[:, (tt % 2) * 2:(tt % 2) * 2 + 2],
                                  afbn[:, tt * NOS * 6:(tt + 1) * NOS * 6])
                    bump(VE, i)
                    v.wait_ge(VE.h, VE.n)
                    i = v.tensor_copy(mu[:, tt:tt + 1], agg8[:, (tt % 2) * 2:(tt % 2) * 2 + 1])
                    bump(VE, i)
                    v.wait_ge(VE.h, VE.n)
                    i = v.tensor_scalar(sgl8[:, tt * 3:tt * 3 + 1],
                                        agg8[:, (tt % 2) * 2 + 1:(tt % 2) * 2 + 2], EPS, None, ALU.add)
                    bump(VE, i)
                ve_v = VE.n

                @A
                def _(a, tt=tt, ve_v=ve_v):
                    a.wait_ge(VE.h, ve_v)
                    i = a.activation(sgl8[:, tt * 3 + 1:tt * 3 + 2],
                                     sgl8[:, tt * 3:tt * 3 + 1], AF.Ln, scale=1.0)
                    bump(AC, i)
                    a.wait_ge(AC.h, AC.n)
                    i = a.activation(rst[:, tt:tt + 1], sgl8[:, tt * 3 + 1:tt * 3 + 2],
                                     AF.Exp, scale=-0.5)
                    bump(AC, i)
                ac_done[tt] = AC.n
            return fin

        def p3_guard(os_):
            def guard(g, os_=os_):
                if os_ <= 1:
                    s_, sv = tree_done[2 + os_]
                    g.wait_ge(s_.h, sv)
                else:
                    g.wait_ge(TG.h, tg_os_done[("m", os_ - 2)])
            return guard

        ac_astat = {}
        a_fin = stats_fin_maker(amu, arst, ac_astat)
        emit_slab_load("m", wmT, 0, p3_guard(0))
        emit_mask_mult(0, split=True)
        emit_slab_load("m", wmT, 1, p3_guard(1))
        emit_mask_mult(1, split=True)
        pend = []
        for tt in range(NT):
            if tt + 6 < NT:
                emit_x_tile(tt + 6)
            for e in pend:
                e()
            pend = []
            emit_group("m", 0, tt, None, a_fin, a_dram, ac_h_done, defer=pend)
            emit_group("m", 1, tt, None, a_fin, a_dram, ac_h_done, defer=pend)
        for e in pend:
            e()
        pend = []
        for os_ in range(2, NOS):
            emit_slab_load("m", wmT, os_, p3_guard(os_))
            emit_mask_mult(os_)
            for tt in range(NT):
                emit_group("m", os_, tt, None, a_fin, a_dram, ac_h_done)
        tg_p3_end = TG.n
        css_p3 = (CSS[0].n, CSS[1].n)
        csf_p3 = [CSF[i].n for i in range(NT)]

        @G
        def _(g):
            g.wait_ge(TG.h, tg_p3_end)
            g.wait_ge(TT.h, tt_sel_end)
            g.dma_start(r1[:, :], ga[:, :]).then_inc(LGC.h, 16)
            g.dma_start(r2[:, :], bcat[:, :]).then_inc(LGC.h, 16)
        LGC.n += 32
        lg_ffn = LGC.n

        def p6_guard(os_):
            def guard(g, os_=os_):
                if os_ <= 1:
                    g.wait_ge(TG.h, tg_p3_end)
                else:
                    g.wait_ge(TG.h, tg_os_done[("b", os_ - 2)])
            return guard

        emit_slab_load("b", wbT, 0, p6_guard(0))
        emit_slab_load("b", wbT, 1, p6_guard(1))

        ve_lna = {}
        ve_h2ap = {}
        cs_x2 = {}
        ve_smx_done = {}
        ac_h2_done = {}
        po_x2 = {}
        p5_state = {}

        def p5_stage_a(tt):
            xb = xbuf[tt % 2]
            lsx = LSX[tt % 2]
            tp = tmp[tt % 2]

            @S
            def _(s, tt=tt, xb=xb, lsx=lsx):
                if tt >= 2:
                    cx = cs_x2[tt - 2]
                    s.wait_ge(cx[0].h, cx[1])
                    s.wait_ge(VE.h, ve_h2ap[tt - 2])
                else:
                    s.wait_ge(VE.h, ve_xfree[NT - 2 + tt])
                s.dma_start(xb[:, :], xin[tt * 128:(tt + 1) * 128, :]).then_inc(lsx.h, 16)
                if tt >= 1:
                    s.wait_ge(VE.h, ve_lna[tt - 1])
                s.wait_ge(CSS[0].h, css_p3[0])
                s.wait_ge(CSF[tt].h, csf_p3[tt])
                s.dma_start(afin[:, :], a_dram[tt * 128:(tt + 1) * 128, :]).then_inc(LSA.h, 16)
            lsx.n += 16
            LSA.n += 16
            ls_x, ls_a = lsx.n, LSA.n

            @V
            def _(v, tt=tt, tp=tp, ls_a=ls_a, guards=tuple(tmp_guard[tt % 2])):
                v.wait_ge(LSA.h, ls_a)
                v.wait_ge(AC.h, ac_astat[tt])
                for (sh, sv) in guards:
                    v.wait_ge(sh, sv)
                i = v.tensor_scalar(tp[:, :], afin[:, :], amu[:, tt:tt + 1],
                                    arst[:, tt:tt + 1], ALU.subtract, ALU.mult)
                bump(VE, i)
            ve_lna[tt] = VE.n

            @G
            def _(g, tt=tt, xb=xb, tp=tp, lsx=lsx, ls_x=ls_x):
                g.wait_ge(lsx.h, ls_x)
                g.wait_ge(VE.h, ve_lna[tt])
                i = g.tensor_tensor(xb[:, 0:768], xb[:, 0:768], tp[:, 0:768], ALU.add)
                bump(PO, i)
            po_x2[tt] = PO.n

            @V
            def _(v, tt=tt, xb=xb, tp=tp, lsx=lsx, ls_x=ls_x):
                v.wait_ge(lsx.h, ls_x)
                v.wait_ge(VE.h, ve_lna[tt])
                i = v.tensor_tensor(xb[:, 768:4096], xb[:, 768:4096],
                                    tp[:, 768:4096], ALU.add)
                bump(VE, i)
            ve_x2b = VE.n

            csx = CSX[tt % 2]

            @A
            def _(a, tt=tt, xb=xb, csx=csx, ve_x2b=ve_x2b):
                a.wait_ge(PO.h, po_x2[tt])
                a.wait_ge(VE.h, ve_x2b)
                a.dma_start(x2d[tt * 128:(tt + 1) * 128, :], xb[:, :]).then_inc(csx.h, 16)
            csx.n += 16
            cs_x2[tt] = (csx, csx.n)

            @V
            def _(v, tt=tt, xb=xb, ve_x2b=ve_x2b):
                v.wait_ge(PO.h, po_x2[tt])
                v.wait_ge(VE.h, ve_x2b)
                sb0 = (tt % 2) * 48
                for c in range(8):
                    i = v.bn_stats(stat8[:, sb0 + c * 6:sb0 + (c + 1) * 6],
                                   xb[:, c * 512:(c + 1) * 512])
                    bump(VE, i)
                v.wait_ge(VE.h, VE.n)
                i = v.bn_aggr(agg8[:, (tt % 2) * 2:(tt % 2) * 2 + 2], stat8[:, sb0:sb0 + 48])
                bump(VE, i)
                v.wait_ge(VE.h, VE.n)
                i = v.tensor_scalar(sgl8[:, tt * 3:tt * 3 + 1],
                                    agg8[:, (tt % 2) * 2 + 1:(tt % 2) * 2 + 2], EPS, None, ALU.add)
                bump(VE, i)
            ve_var2 = VE.n

            @A
            def _(a, tt=tt, ve_var2=ve_var2):
                a.wait_ge(VE.h, ve_var2)
                i = a.activation(sgl8[:, tt * 3 + 1:tt * 3 + 2],
                                 sgl8[:, tt * 3:tt * 3 + 1], AF.Ln, scale=1.0)
                bump(AC, i)
                a.wait_ge(AC.h, AC.n)
                i = a.activation(sgl8[:, tt * 3 + 2:tt * 3 + 3],
                                 sgl8[:, tt * 3 + 1:tt * 3 + 2], AF.Exp, scale=-0.5)
                bump(AC, i)
            ac_r2 = AC.n

            @V
            def _(v, tt=tt, xb=xb, tp=tp, ac_r2=ac_r2):
                v.wait_ge(AC.h, ac_r2)
                v.wait_ge(PO.h, po_x2[tt])
                i = v.tensor_scalar(tp[:, :], xb[:, :], agg8[:, (tt % 2) * 2:(tt % 2) * 2 + 1],
                                    sgl8[:, tt * 3 + 2:tt * 3 + 3],
                                    ALU.subtract, ALU.mult)
                bump(VE, i)
            ve_h2ap[tt] = VE.n
            ve_h2 = VE.n

            for b4 in range(8):
                kc0 = b4 * 4
                srcs = [tp[:, (kc0 + j) * 128:(kc0 + j + 1) * 128] for j in range(4)]

                def writer(a, ps, tt=tt, kc0=kc0):
                    out_ap = bass.AP(hbuf, kc0 * 1024 + tt * 128,
                                     [[hbuf.shape[1], 128], [1024, 4], [1, 128]])
                    in_ap = bass.AP(ps, 0, [[ps.shape[1], 128], [128, 4], [1, 128]])
                    if hasattr(a, "copy"):
                        return a.copy(out_ap, in_ap)
                    return a.tensor_copy(out_ap, in_ap)

                transpose_batch(srcs, writer, [(VE.h, ve_h2)],
                                evac_engine=("V" if b4 % 2 else None))
            tmp_guard[tt % 2] = [(TT.h, TT.n)]
            ac_h2_done[tt] = [(AC.h, AC.n), (VE.h, VE.n)]

        def p5_stage_b(tt):
            sp = (tt % 2) * 28
            hp = Hp[:, (tt % 2) * 128:(tt % 2) * 128 + 128]
            psh = psH[tt % 2]

            @T
            def _(t, tt=tt, psh=psh):
                for (sh, sv) in ac_h2_done[tt]:
                    t.wait_ge(sh, sv)
                t.wait_ge(LGC.h, lg_ffn)
                t.wait_ge(ONE.h, 1)
                if tt >= 2:
                    t.wait_ge(VE.h, ve_smx_done[tt - 2])
                for kc in range(KC):
                    t.matmul(psh[:, :],
                             hbuf[:, kc * 1024 + tt * 128:kc * 1024 + (tt + 1) * 128],
                             r1[:, kc * 144:(kc + 1) * 144],
                             start=(kc == 0), stop=False)
                mm = t.matmul(psh[:, :], onesb[:, :], gabt[0:1, :], start=False, stop=True)
                bump(TG, mm)
            tg_gate = TG.n

            @V
            def _(v, tt=tt, psh=psh, tg_gate=tg_gate):
                v.wait_ge(TG.h, tg_gate)
                i = v.reduce_max(nmb[:, (tt % 2):(tt % 2) + 1], psh[:, 128:136],
                                 axis=AXX, negate=True)
                bump(VE, i)
            ve_nm = VE.n

            @A
            def _(a, tt=tt, sp=sp, psh=psh, ve_nm=ve_nm):
                a.wait_ge(VE.h, ve_nm)
                i = a.activation(smx[:, sp + 1:sp + 9], psh[:, 128:136], AF.Exp,
                                 bias=nmb[:, (tt % 2):(tt % 2) + 1], scale=1.0)
                bump(AC, i)
                a.wait_ge(AC.h, AC.n)
                i = a.activation(smx[:, sp + 9:sp + 10], psh[:, 136:137], AF.Exp, scale=-1.0)
                bump(AC, i)
            ac_exp = AC.n

            @V
            def _(v, tt=tt, sp=sp, hp=hp, psh=psh, ac_exp=ac_exp):
                v.wait_ge(AC.h, ac_exp)
                i = v.reduce_sum(smx[:, sp + 10:sp + 11], smx[:, sp + 1:sp + 9], axis=AXX); bump(VE, i)
                v.wait_ge(VE.h, VE.n)
                i = v.reciprocal(smx[:, sp + 11:sp + 12], smx[:, sp + 10:sp + 11]); bump(VE, i)
                i = v.tensor_scalar(smx[:, sp + 12:sp + 13], smx[:, sp + 9:sp + 10],
                                    1.0, None, ALU.add); bump(VE, i)
                v.wait_ge(VE.h, VE.n)
                i = v.reciprocal(smx[:, sp + 13:sp + 14], smx[:, sp + 12:sp + 13]); bump(VE, i)
                v.wait_ge(VE.h, VE.n)
                i = v.tensor_scalar(smx[:, sp + 14:sp + 15], smx[:, sp + 13:sp + 14],
                                    MAXTHR, None, ALU.mult); bump(VE, i)
                v.wait_ge(VE.h, VE.n)
                i = v.tensor_scalar(smx[:, sp + 16:sp + 24], smx[:, sp + 1:sp + 9],
                                    smx[:, sp + 11:sp + 12], smx[:, sp + 14:sp + 15],
                                    ALU.mult, ALU.subtract); bump(VE, i)
                v.wait_ge(VE.h, VE.n)
                i = v.tensor_scalar(smx[:, sp + 16:sp + 24], smx[:, sp + 16:sp + 24],
                                    0.0, None, ALU.max); bump(VE, i)
                v.wait_ge(VE.h, VE.n)
                i = v.reduce_sum(smx[:, sp + 24:sp + 25], smx[:, sp + 16:sp + 24], axis=AXX); bump(VE, i)
                v.wait_ge(VE.h, VE.n)
                i = v.tensor_scalar(nmb[:, 2 + tt % 2:3 + tt % 2], smx[:, sp + 24:sp + 25],
                                    0.0, None, ALU.is_equal); bump(VE, i)
                v.wait_ge(VE.h, VE.n)
                i = v.tensor_tensor(smx[:, sp + 26:sp + 27], smx[:, sp + 24:sp + 25],
                                    nmb[:, 2 + tt % 2:3 + tt % 2], ALU.add); bump(VE, i)
                v.wait_ge(VE.h, VE.n)
                i = v.reciprocal(smx[:, sp + 27:sp + 28], smx[:, sp + 26:sp + 27]); bump(VE, i)
                v.wait_ge(VE.h, VE.n)
                i = v.tensor_scalar(smx[:, sp + 16:sp + 24], smx[:, sp + 16:sp + 24],
                                    smx[:, sp + 27:sp + 28], SCALING,
                                    ALU.mult, ALU.mult); bump(VE, i)
                v.wait_ge(VE.h, VE.n)
                for e in range(8):
                    i = v.tensor_scalar(hp[:, e * 16:(e + 1) * 16], psh[:, e * 16:(e + 1) * 16],
                                        smx[:, sp + 16 + e:sp + 17 + e], None, ALU.mult)
                    bump(VE, i)
            ve_hp = VE.n

            def writerH(a, ps, tt=tt):
                return a.copy(HT[:, tt * 128:(tt + 1) * 128], ps[:, 0:128])

            transpose_batch([hp], writerH, [(VE.h, ve_hp)])

            @V
            def _(v):
                i = v.engine_nop()
                bump(VE, i)
            ve_smx_done[tt] = VE.n
            ac_h2_done[tt] = [(AC.h, AC.n), (VE.h, VE.n)]

            def moe0(t, ps, tt=tt):
                t.matmul(ps[:, :], HT[:, tt * 128:(tt + 1) * 128],
                         r2[:, 0:OSW], start=False, stop=False)
            emit_group("b", 0, tt, moe0, None, f_dram, ac_h2_done, defer=p5_pend)

            def moe1(t, ps, tt=tt):
                t.matmul(ps[:, :], HT[:, tt * 128:(tt + 1) * 128],
                         r2[:, OSW:2 * OSW], start=False, stop=False)
            emit_group("b", 1, tt, moe1, None, f_dram, ac_h2_done, defer=p5_pend)

        p5_pend = []
        for tt in range(NT + 1):
            for e in list(p5_pend):
                e()
            p5_pend.clear()
            if tt >= 1:
                p5_stage_b(tt - 1)
            if tt < NT:
                p5_stage_a(tt)
        for e in list(p5_pend):
            e()
        p5_pend.clear()

        csx_p5 = (CSX[0].n, CSX[1].n)

        ac_fstat = {}
        f_fin = stats_fin_maker(fmu, frst, ac_fstat)
        ve_lnf = {}
        cs_out = {}
        po_out = {}

        ve_add = {}
        cs_out = {}

        def emit_p8_tile(tt):
            xb = xbuf[tt % 2]
            lsx = LSX[tt % 2]
            tp = tmp[tt % 2]
            css0_v = CSS[0].n
            lsf = LSF[tt % 2]
            W7 = (NOS - 1) * OSW
            fbuf = slab[0][:, (tt % 2) * D:(tt % 2) * D + W7]

            @S
            def _(s, tt=tt, xb=xb, lsx=lsx, css0_v=css0_v,
                  lsf=lsf, fbuf=fbuf, W7=W7):
                s.wait_ge(CSS[0].h, css0_v)
                s.wait_ge(CSX[0].h, csx_p5[0])
                s.wait_ge(CSX[1].h, csx_p5[1])
                if tt >= 2:
                    co = cs_out[tt - 2]
                    s.wait_ge(co[0].h, co[1])
                s.dma_start(xb[:, :], x2d[tt * 128:(tt + 1) * 128, :]).then_inc(lsx.h, 16)
                s.wait_ge(TG.h, tg_os_done[("b", NOS - 2)])
                if tt >= 2:
                    s.wait_ge(VE.h, ve_lnf[tt - 2])
                s.dma_start(fbuf, f_dram[tt * 128:(tt + 1) * 128, 0:W7]).then_inc(lsf.h, 16)
            lsx.n += 16
            lsf.n += 16
            ls_x, ls_f = lsx.n, lsf.n

            st7, ve7 = stg7[tt]

            @V
            def _(v, tt=tt, tp=tp, ls_f=ls_f, lsf=lsf, fbuf=fbuf, W7=W7,
                  st7=st7, ve7=ve7, guards=tuple(tmp_guard[tt % 2])):
                v.wait_ge(lsf.h, ls_f)
                v.wait_ge(AC.h, ac_fstat[tt])
                v.wait_ge(VE.h, ve7)
                for (sh, sv) in guards:
                    v.wait_ge(sh, sv)
                i = v.tensor_scalar(tp[:, 0:W7], fbuf, fmu[:, tt:tt + 1],
                                    frst[:, tt:tt + 1], ALU.subtract, ALU.mult)
                bump(VE, i)
                i = v.tensor_scalar(tp[:, W7:D], st7, fmu[:, tt:tt + 1],
                                    frst[:, tt:tt + 1], ALU.subtract, ALU.mult)
                bump(VE, i)
            ve_lnf[tt] = VE.n

            @G
            def _(g, tt=tt, xb=xb, tp=tp, lsx=lsx, ls_x=ls_x):
                g.wait_ge(lsx.h, ls_x)
                g.wait_ge(VE.h, ve_lnf[tt])
                i = g.tensor_tensor(xb[:, 0:768], xb[:, 0:768], tp[:, 0:768], ALU.add)
                bump(PO, i)
            po_out[tt] = PO.n

            @V
            def _(v, tt=tt, xb=xb, tp=tp, lsx=lsx, ls_x=ls_x):
                v.wait_ge(lsx.h, ls_x)
                v.wait_ge(VE.h, ve_lnf[tt])
                i = v.tensor_tensor(xb[:, 768:4096], xb[:, 768:4096],
                                    tp[:, 768:4096], ALU.add)
                bump(VE, i)
            ve_add[tt] = VE.n
            tmp_guard[tt % 2] = [(PO.h, PO.n), (VE.h, ve_add[tt])]

            csx = CSX[tt % 2]

            @A
            def _(a, tt=tt, xb=xb, csx=csx):
                a.wait_ge(PO.h, po_out[tt])
                a.wait_ge(VE.h, ve_add[tt])
                if tt == NT - 1:
                    a.dma_start(outd[tt * 128:(tt + 1) * 128, 0:2048],
                                xb[:, 0:2048]).then_inc(csx.h, 16)
                else:
                    a.dma_start(outd[tt * 128:(tt + 1) * 128, :],
                                xb[:, :]).then_inc(csx.h, 16)
            csx.n += 16

            if tt == NT - 1:
                @S
                def _(s, tt=tt, xb=xb, csx=csx):
                    s.wait_ge(PO.h, po_out[tt])
                    s.wait_ge(VE.h, ve_add[tt])
                    s.dma_start(outd[tt * 128:(tt + 1) * 128, 2048:4096],
                                xb[:, 2048:4096]).then_inc(csx.h, 16)
                csx.n += 16
            cs_out[tt] = (csx, csx.n)

        stg7 = {}
        slot7 = {}
        pend7 = []

        def flush7(tt_prev):
            for e in pend7:
                e()
            pend7.clear()
            stg7[tt_prev] = (slot7[tt_prev], VE.n)

        for os_ in range(2, NOS):
            emit_slab_load("b", wbT, os_, p6_guard(os_))
            for tt in range(NT):
                def moe(t, ps, os_=os_, tt=tt):
                    t.matmul(ps[:, :], HT[:, tt * 128:(tt + 1) * 128],
                             r2[:, os_ * OSW:(os_ + 1) * OSW], start=False, stop=False)
                if os_ == NOS - 1:
                    if tt >= 1:
                        flush7(tt - 1)
                    slot7[tt] = slab[0][:, 2 * D + (tt % 2) * OSW:
                                        2 * D + (tt % 2) * OSW + OSW]
                    emit_group("b", os_, tt, moe, None, f_dram, ac_h2_done,
                               stg_slot=slot7[tt], pre_ve=ve_lnf.get(tt - 2),
                               defer=pend7)
                else:
                    emit_group("b", os_, tt, moe, f_fin, f_dram, ac_h2_done)
                if os_ == NOS - 1 and tt >= 1:
                    f_fin(tt - 1)
                    emit_p8_tile(tt - 1)
        flush7(NT - 1)
        f_fin(NT - 1)
        emit_p8_tile(NT - 1)
        css_p6 = (CSS[0].n, CSS[1].n)

        @A
        def _(a):
            a.wait_ge(CSX[0].h, CSX[0].n)
            a.wait_ge(CSX[1].h, CSX[1].n)

        @S
        def _(s):
            s.wait_ge(CSX[0].h, CSX[0].n)
            s.wait_ge(CSX[1].h, CSX[1].n)

    return nc


_NC_CACHE = None


def _host_pack(inputs):
    L0_attn_data = np.asarray(inputs["L0_attn_data"], dtype=np.float32)
    mask = np.ascontiguousarray(np.asarray(inputs["L0_attn_mask"], dtype=np.int32))
    weight = np.asarray(inputs["L0_attn_weight"], dtype=np.float32)
    battn = np.asarray(inputs["L0_attn_bias"], dtype=np.float32)
    W_base = np.asarray(inputs["L0_ffn_W_base"], dtype=np.float32)
    b_base = np.asarray(inputs["L0_ffn_b_base"], dtype=np.float32)
    W_gate = np.asarray(inputs["L0_ffn_W_gate"], dtype=np.float32)
    W_thr = np.asarray(inputs["L0_ffn_W_thr"], dtype=np.float32)
    b_thr = np.asarray(inputs["L0_ffn_b_thr"], dtype=np.float32)
    A_w = np.asarray(inputs["L0_ffn_A_w"], dtype=np.float32)
    B_w = np.asarray(inputs["L0_ffn_B_w"], dtype=np.float32)

    x8 = np.ascontiguousarray(L0_attn_data.reshape(NCORE * TPC, D))
    wmT = np.ascontiguousarray(weight.T)
    wbT = np.ascontiguousarray(W_base.T)

    AcatT = A_w.transpose(2, 0, 1).reshape(D, 128)
    gcols = np.concatenate([W_gate.T, W_thr[:, None]], axis=1)
    gap = np.zeros((128, KC * 144), dtype=np.float32)
    for kc in range(KC):
        gap[:, kc * 144:kc * 144 + 128] = AcatT[kc * 128:(kc + 1) * 128, :]
        gap[:, kc * 144 + 128:kc * 144 + 137] = gcols[kc * 128:(kc + 1) * 128, :]
    gab = np.zeros((1, 144), dtype=np.float32)
    gab[0, 136] = b_thr[0]
    bcat = np.ascontiguousarray(B_w.transpose(0, 2, 1).reshape(128, D))

    kb = np.arange(128)[:, None]
    p = np.arange(128)[None, :]
    selp = np.zeros((128, KC * 128), dtype=BF)
    for kc in range(KC):
        selp[:, kc * 128:(kc + 1) * 128] = (kb == 4 * kc + p // 32).astype(BF)
    idnp = np.eye(128, dtype=BF)

    shared = {
        "wmT": wmT, "wbT": wbT, "ga": gap, "gab": gab,
        "bcat": bcat, "battn": np.ascontiguousarray(battn[None, :]),
        "bbase": np.ascontiguousarray(b_base[None, :]),
        "selp": selp, "idnp": idnp,
    }
    in_maps = []
    for c in range(NCORE):
        m = dict(shared)
        m["xin"] = np.ascontiguousarray(x8[c * TPC:(c + 1) * TPC, :])
        m["maskN"] = np.ascontiguousarray(mask[c * (D // 8):(c + 1) * (D // 8), :])
        in_maps.append(m)
    return in_maps


def kernel(**inputs):
    global _NC_CACHE
    in_maps = _host_pack(inputs)
    if _NC_CACHE is None:
        _NC_CACHE = build_nc()
    res = run_bass_kernel_spmd(_NC_CACHE, in_maps, core_ids=list(range(NCORE)))
    outs = [np.asarray(res.results[c]["outd"]) for c in range(NCORE)]
    full = np.concatenate(outs, axis=0).reshape(4, 2048, D).astype(np.float32)
    return full


# version 28 (final)
# speedup vs baseline: 1.0522x; 1.0082x over previous
import numpy as np
import ml_dtypes
import concourse.bass as bass
import concourse.mybir as mybir
from concourse.alu_op_type import AluOpType as ALU
from concourse.bass_utils import run_bass_kernel_spmd
from contextlib import ExitStack

F32 = mybir.dt.float32
BF16 = mybir.dt.bfloat16
I32 = mybir.dt.int32
AF = mybir.ActivationFunctionType
AXX = mybir.AxisListType.X
BF = ml_dtypes.bfloat16

NCORE = 8
D = 4096
TPC = 1024
NT = TPC // 128
KC = D // 128
OSW = 512
NOS = D // OSW
EPS = 1e-5
MAXTHR = 0.125
SCALING = 2.0


class Sem:
    __slots__ = ("h", "n")

    def __init__(self, h):
        self.h = h
        self.n = 0


def build_nc():
    nc = bass.Bass("TRN2", target_bir_lowering=False)

    xin = nc.dram_tensor("xin", [TPC, D], F32, kind="ExternalInput")
    wmT = nc.dram_tensor("wmT", [D, D], F32, kind="ExternalInput")
    wbT = nc.dram_tensor("wbT", [D, D], F32, kind="ExternalInput")
    maskN = nc.dram_tensor("maskN", [D // 8, D], I32, kind="ExternalInput")
    btin = nc.dram_tensor("btin", [128, 16], F32)
    btout = nc.dram_tensor("btout", [1024, 16], F32)
    ga = nc.dram_tensor("ga", [128, KC * 144], F32, kind="ExternalInput")
    gab = nc.dram_tensor("gab", [1, 144], F32, kind="ExternalInput")
    bcat = nc.dram_tensor("bcat", [128, D], F32, kind="ExternalInput")
    battn = nc.dram_tensor("battn", [1, D], F32, kind="ExternalInput")
    bbase = nc.dram_tensor("bbase", [1, D], F32, kind="ExternalInput")
    selp = nc.dram_tensor("selp", [128, KC * 128], BF16, kind="ExternalInput")
    idnp = nc.dram_tensor("idnp", [128, 128], BF16, kind="ExternalInput")
    x2d = nc.dram_tensor("x2d", [TPC, D], F32)
    a_dram = nc.dram_tensor("a_dram", [TPC, D], BF16)
    f_dram = nc.dram_tensor("f_dram", [TPC, D], BF16)
    outd = nc.dram_tensor("outd", [TPC, D], F32, kind="ExternalOutput")

    es = ExitStack()
    sb = lambda n, s, d: es.enter_context(nc.sbuf_tensor(n, s, d))
    psb = lambda n, s, d: es.enter_context(nc.psum_tensor(n, s, d))

    hbuf = sb("hbuf", [128, KC * 1024], BF16)
    slab = [sb("slabA", [128, KC * OSW], BF16),
            sb("slabB", [128, KC * OSW], BF16)]
    xbuf = [sb("xbufA", [128, D], F32), sb("xbufB", [128, D], F32)]
    tmp = [sb("tmpA", [128, D], BF16), sb("tmpB", [128, D], BF16)]
    r1 = sb("r1", [128, KC * 144], BF16)
    r2 = sb("r2", [128, KC * 128], BF16)
    afin = sb("afin", [128, D], BF16)
    stg = sb("stgA", [128, OSW], BF16)
    btile = sb("btile", [1, 1024], BF16)
    gabt = sb("gabt", [1, 144], BF16)
    idn = sb("idn", [128, 128], BF16)
    onesb = sb("onesb", [1, 128], BF16)
    BTf = sb("BTf", [128, 128], F32)
    nmb = sb("nmb", [128, 4], F32)
    Hp = sb("Hp", [128, 2 * 128], BF16)
    HT = sb("HT", [128, NT * 128], BF16)

    psA = psb("psA", [128, OSW], F32)
    psB = psb("psB", [128, OSW], F32)
    psT = [psb("psT0", [128, 512], BF16), psb("psT1", [128, 512], BF16)]
    psH = [psb("psH0", [128, 144], F32), psb("psH1", [128, 144], F32)]
    aux = psb("aux", [128, 196], F32)
    afbn = psb("afbn", [128, NT * NOS * 6], F32)

    class _View:
        __slots__ = ("t", "off")

        def __init__(self, t, off):
            self.t = t
            self.off = off

        def __getitem__(self, idx):
            rows, cols = idx
            return self.t[rows, cols.start + self.off:cols.stop + self.off]

    stat8 = _View(aux, 0)
    agg8 = _View(aux, 96)
    sgl8 = _View(aux, 100)
    amu = _View(aux, 124)
    arst = _View(aux, 132)
    fmu = amu
    frst = arst
    smx = _View(aux, 140)
    psmm = [psA, psB]

    LGC = Sem(es.enter_context(nc.semaphore("LGC")))
    LGM = [Sem(es.enter_context(nc.semaphore(f"LGM{i}"))) for i in range(8)]
    LGW = [Sem(es.enter_context(nc.semaphore("LGW0"))),
           Sem(es.enter_context(nc.semaphore("LGW1")))]
    LSX = [Sem(es.enter_context(nc.semaphore("LSX0"))),
           Sem(es.enter_context(nc.semaphore("LSX1")))]
    LSA = Sem(es.enter_context(nc.semaphore("LSA")))
    LSF = [Sem(es.enter_context(nc.semaphore("LSF0"))),
           Sem(es.enter_context(nc.semaphore("LSF1")))]
    CSS = [Sem(es.enter_context(nc.semaphore("CSS0"))),
           Sem(es.enter_context(nc.semaphore("CSS1")))]
    CSX = [Sem(es.enter_context(nc.semaphore("CSX0"))),
           Sem(es.enter_context(nc.semaphore("CSX1")))]
    CSF = [Sem(es.enter_context(nc.semaphore(f"CSF{i}"))) for i in range(NT)]
    CCS = Sem(es.enter_context(nc.semaphore("CCS")))
    CCD = Sem(es.enter_context(nc.semaphore("CCD")))
    ONE = Sem(es.enter_context(nc.semaphore("ONE")))
    TG = Sem(es.enter_context(nc.semaphore("TG")))
    TT = Sem(es.enter_context(nc.semaphore("TT")))
    VE = Sem(es.enter_context(nc.semaphore("VE")))
    AC = Sem(es.enter_context(nc.semaphore("AC")))
    PO = Sem(es.enter_context(nc.semaphore("PO")))

    block = es.enter_context(nc.Block())

    def T(f): block.tensor(f)
    def V(f): block.vector(f)
    def A(f): block.scalar(f)
    def G(f): block.gpsimd(f)
    def S(f): block.sync(f)

    def bump(sem, instr, k=1):
        instr.then_inc(sem.h, k)
        sem.n += k
        return sem.n

    def pair_halve(eng, src_t, src_off, dst_ap, w, row_span):
        in0 = bass.AP(src_t, src_off, [[row_span, 128], [2, w]])
        in1 = bass.AP(src_t, src_off + 1, [[row_span, 128], [2, w]])
        return eng.tensor_tensor(dst_ap, in0, in1, ALU.add)

    tb = [0]
    ac_evac = {}
    tmp_guard = {0: [], 1: []}

    def transpose_batch(srcs, writer, extra_tensor_waits=(), evac_engine=None):
        b = tb[0]
        par = b % 2
        prev = ac_evac.get(b - 2)

        @T
        def _(t, b=b, par=par, prev=prev, srcs=srcs, xw=tuple(extra_tensor_waits)):
            for (sh, sv) in xw:
                t.wait_ge(sh, sv)
            if prev:
                t.wait_ge(prev[0], prev[1])
            mm = None
            for j, sap in enumerate(srcs):
                mm = t.transpose(psT[par][:, j * 128:(j + 1) * 128], sap, idn[:, :])
            bump(TT, mm)
        ttv = TT.n

        use_v = (evac_engine == "V")
        if use_v:
            @V
            def _(v, par=par, ttv=ttv, writer=writer):
                v.wait_ge(TT.h, ttv)
                i = writer(v, psT[par])
                bump(VE, i)
            ac_evac[b] = (VE.h, VE.n)
            ev = (VE.h, VE.n)
        else:
            @A
            def _(a, par=par, ttv=ttv, writer=writer):
                a.wait_ge(TT.h, ttv)
                i = writer(a, psT[par])
                bump(AC, i)
            ac_evac[b] = (AC.h, AC.n)
            ev = (AC.h, AC.n)
        tb[0] += 1
        return ttv, ev

    with es:
        @A
        def _(a):
            a.add_instruction(mybir.InstLoadActFuncSet(
                name=a.bass.get_next_instruction_name(),
                act_func_set_id=6, ins=[], outs=[]))

        @G
        def _(g):
            g.dma_start(idn[:, :], idnp[:, :]).then_inc(LGC.h, 16)
            g.dma_start(r2[:, :], selp[:, :]).then_inc(LGC.h, 16)
            g.dma_start(gabt[:, :], gab[:, :]).then_inc(LGC.h, 16)
            g.memset(onesb[:, :], 1.0).then_inc(ONE.h, 1)
        LGC.n += 48
        ONE.n += 1
        p0_lg = LGC.n

        ve_xfree = {}
        ac_rstd = {}
        ac_h_done = {}

        def emit_x_tile(tt):
            xb = xbuf[tt % 2]
            lsx = LSX[tt % 2]
            tp = tmp[tt % 2]

            @S
            def _(s, tt=tt, xb=xb, lsx=lsx):
                if tt >= 2:
                    s.wait_ge(VE.h, ve_xfree[tt - 2])
                s.dma_start(xb[:, :], xin[tt * 128:(tt + 1) * 128, :]).then_inc(lsx.h, 16)
            lsx.n += 16
            ls_x = lsx.n

            @V
            def _(v, tt=tt, xb=xb, lsx=lsx, ls_x=ls_x):
                v.wait_ge(lsx.h, ls_x)
                sb0 = (tt % 2) * 48
                for c in range(8):
                    i = v.bn_stats(stat8[:, sb0 + c * 6:sb0 + (c + 1) * 6],
                                   xb[:, c * 512:(c + 1) * 512])
                    bump(VE, i)
                v.wait_ge(VE.h, VE.n)
                i = v.bn_aggr(agg8[:, (tt % 2) * 2:(tt % 2) * 2 + 2], stat8[:, sb0:sb0 + 48])
                bump(VE, i)
                v.wait_ge(VE.h, VE.n)
                i = v.tensor_scalar(sgl8[:, tt * 3:tt * 3 + 1],
                                    agg8[:, (tt % 2) * 2 + 1:(tt % 2) * 2 + 2], EPS, None, ALU.add)
                bump(VE, i)
            ve_var = VE.n

            @A
            def _(a, tt=tt, ve_var=ve_var):
                a.wait_ge(VE.h, ve_var)
                i = a.activation(sgl8[:, tt * 3 + 1:tt * 3 + 2],
                                 sgl8[:, tt * 3:tt * 3 + 1], AF.Ln, scale=1.0)
                bump(AC, i)
                a.wait_ge(AC.h, AC.n)
                i = a.activation(sgl8[:, tt * 3 + 2:tt * 3 + 3],
                                 sgl8[:, tt * 3 + 1:tt * 3 + 2], AF.Exp, scale=-0.5)
                bump(AC, i)
            ac_rstd[tt] = AC.n

            @V
            def _(v, tt=tt, xb=xb, tp=tp, guards=tuple(tmp_guard[tt % 2])):
                v.wait_ge(AC.h, ac_rstd[tt])
                for (sh, sv) in guards:
                    v.wait_ge(sh, sv)
                i = v.tensor_scalar(tp[:, :], xb[:, :], agg8[:, (tt % 2) * 2:(tt % 2) * 2 + 1],
                                    sgl8[:, tt * 3 + 2:tt * 3 + 3],
                                    ALU.subtract, ALU.mult)
                bump(VE, i)
            ve_xfree[tt] = VE.n
            ve_h1 = VE.n

            for b4 in range(8):
                kc0 = b4 * 4
                srcs = [tp[:, (kc0 + j) * 128:(kc0 + j + 1) * 128] for j in range(4)]

                def writer(a, ps, tt=tt, kc0=kc0):
                    out_ap = bass.AP(hbuf, kc0 * 1024 + tt * 128,
                                     [[hbuf.shape[1], 128], [1024, 4], [1, 128]])
                    in_ap = bass.AP(ps, 0, [[ps.shape[1], 128], [128, 4], [1, 128]])
                    if hasattr(a, "copy"):
                        return a.copy(out_ap, in_ap)
                    return a.tensor_copy(out_ap, in_ap)

                transpose_batch(srcs, writer, [(VE.h, ve_h1)],
                                evac_engine=("V" if b4 % 2 else None))
            tmp_guard[tt % 2] = [(TT.h, TT.n)]
            ac_h_done[tt] = [(AC.h, AC.n), (VE.h, VE.n)]

        tree_done = {}
        stg_free = {}

        def emit_mask_tile(ot):
            mbuf = slab[ot % 2]
            mt_off = ((ot // 2) % 4) * D
            mt = mbuf[:, mt_off:mt_off + D]
            lgm = LGM[ot % 8]

            @G
            def _(g, ot=ot, mt=mt, lgm=lgm):
                if ot >= 8:
                    s_, sv = tree_done[ot - 8]
                    g.wait_ge(s_.h, sv)
                g.dma_start(mt, maskN[ot * 128:(ot + 1) * 128, :]).then_inc(lgm.h, 16)
            lgm.n += 16
            lg_tile = lgm.n

            par = ot % 2
            tp = tmp[par]
            SEM = VE if par == 0 else PO
            EDEC = V if par == 0 else G

            @EDEC
            def _(e, ot=ot, mbuf=mbuf, mt_off=mt_off, lgm=lgm, lg_tile=lg_tile,
                  tp=tp, SEM=SEM, guards=tuple(tmp_guard[par])):
                e.wait_ge(lgm.h, lg_tile)
                if ot >= 2:
                    s_, sv = tree_done[ot - 2]
                    e.wait_ge(s_.h, sv)
                for (sh, sv) in guards:
                    e.wait_ge(sh, sv)
                segs = [(0, 2048), (2048, 1024), (3072, 512), (3584, 256), (3840, 128)]
                src_t, src_off, row_span = mbuf, mt_off, mbuf.shape[1]
                for si, (o0, w) in enumerate(segs):
                    last = pair_halve(e, src_t, src_off, tp[:, o0:o0 + w], w, row_span)
                    bump(SEM, last)
                    e.wait_ge(SEM.h, SEM.n)
                    if si == 0:
                        stg_free[ot] = (SEM, SEM.n)
                    src_t, src_off, row_span = tp, o0, tp.shape[1]
            s1 = SEM.n

            def writer(a, ps, tp=tp):
                return a.copy(tp[:, 128:256], ps[:, 0:128])

            _, ac_ev = transpose_batch([tp[:, 3840:3968]], writer,
                                       [(SEM.h, s1), (LGC.h, p0_lg)])

            @EDEC
            def _(e, ot=ot, tp=tp, SEM=SEM, ac_ev=ac_ev):
                e.wait_ge(ac_ev[0], ac_ev[1])
                segs = [(256, 64), (320, 32), (384, 16), (400, 8), (408, 4)]
                src_t, src_off = tp, 128
                for (o0, w) in segs:
                    last = pair_halve(e, src_t, src_off, tp[:, o0:o0 + w], w, tp.shape[1])
                    bump(SEM, last)
                    e.wait_ge(SEM.h, SEM.n)
                    src_t, src_off = tp, o0
                c = e.tensor_copy(BTf[:, 4 * ot:4 * ot + 4], tp[:, 408:412])
                bump(SEM, c)
            tree_done[ot] = (SEM, SEM.n)
            tmp_guard[par] = [(SEM.h, SEM.n)]

        def wslab_dma(g, dram, os_, sub, dst):
            kc0 = sub * 8
            src = bass.AP(dram, kc0 * 128 * D + os_ * OSW,
                          [[D, 128], [128 * D, 8], [1, OSW]])
            dstv = bass.AP(dst, kc0 * OSW,
                           [[dst.shape[1], 128], [OSW, 8], [1, OSW]])
            return g.dma_start(dstv, src)

        tg_p3_0 = TG.n
        lg_slab = {}
        cs_stg = {}
        ve_ev = {}
        po_mask = {}
        gidx = [0]

        def emit_slab_load(key, dram, os_, guard):
            lgw = LGW[os_ % 2]
            bias_src = battn if key == "m" else bbase

            @G
            def _(g, os_=os_, lgw=lgw, guard=guard, bias_src=bias_src, dram=dram):
                guard(g)
                sl = slab[os_ % 2]
                for sub in range(4):
                    wslab_dma(g, dram, os_, sub, sl).then_inc(lgw.h, 16)
                g.dma_start(btile[0:1, (os_ % 2) * 512:(os_ % 2) * 512 + 512],
                            bias_src[0:1, os_ * OSW:(os_ + 1) * OSW]).then_inc(lgw.h, 16)
            lgw.n += 80
            lg_slab[(key, os_)] = (lgw, lgw.n)

        mask_done = {}

        def _mask_sub_aps(sl, os_, sub):
            kc0 = sub * 8
            out_ap = bass.AP(sl, kc0 * OSW,
                             [[sl.shape[1], 128], [OSW, 8], [32, 16], [1, 32]])
            in1 = bass.AP(r1, kc0 * 128 + os_ * 16,
                          [[r1.shape[1], 128], [128, 8], [1, 16], [0, 32]])
            return out_ap, in1

        def emit_mask_mult(os_, split=False):
            lgw, lv = lg_slab[("m", os_)]
            sl = slab[os_ % 2]
            subs_pool = range(2, 4) if split else range(4)

            if split:
                @V
                def _(v, os_=os_, lgw=lgw, lv=lv, sl=sl):
                    v.wait_ge(lgw.h, lv)
                    v.wait_ge(VE.h, ve_pbexp_done)
                    for sub in range(0, 2):
                        out_ap, in1 = _mask_sub_aps(sl, os_, sub)
                        i = v.tensor_tensor(out_ap, out_ap, in1, ALU.mult)
                        bump(VE, i)
                ve_m = VE.n

            @G
            def _(g, os_=os_, lgw=lgw, lv=lv, sl=sl, subs_pool=subs_pool):
                g.wait_ge(lgw.h, lv)
                g.wait_ge(VE.h, ve_pbexp_done)
                for sub in subs_pool:
                    out_ap, in1 = _mask_sub_aps(sl, os_, sub)
                    i = g.tensor_tensor(out_ap, out_ap, in1, ALU.mult)
                    bump(PO, i)
            po_mask[os_] = PO.n
            mask_done[os_] = [(PO.h, PO.n)] + ([(VE.h, ve_m)] if split else [])

        tg_os_done = {}

        def emit_group(key, os_, tt, extra_mm, stats_fin, dram_out, h_ready,
                       defer=None, stg_slot=None, pre_ve=None):
            g_ = gidx[0]
            ps = psmm[g_ % 2]
            sl = slab[os_ % 2]
            css = CSF[tt] if os_ == NOS - 1 else CSS[0]
            st = stg if stg_slot is None else stg_slot

            @T
            def _(t, os_=os_, tt=tt, sl=sl, ps=ps, g_=g_, key=key):
                if key == "m":
                    for (sh, sv) in mask_done[os_]:
                        t.wait_ge(sh, sv)
                else:
                    lgw, lv = lg_slab[("b", os_)]
                    t.wait_ge(lgw.h, lv)
                    t.wait_ge(LGC.h, lg_ffn)
                for (sh, sv) in h_ready[tt]:
                    t.wait_ge(sh, sv)
                t.wait_ge(ONE.h, 1)
                t.wait_ge(LGC.h, p0_lg)
                if g_ >= 2:
                    t.wait_ge(VE.h, ve_ev[g_ - 2])
                for kc in range(KC):
                    t.matmul(ps[:, :],
                             hbuf[:, kc * 1024 + tt * 128:kc * 1024 + (tt + 1) * 128],
                             sl[:, kc * OSW:(kc + 1) * OSW],
                             start=(kc == 0), stop=False)
                if extra_mm is not None:
                    extra_mm(t, ps)
                mm = t.matmul(ps[:, :], onesb[:, :],
                              btile[0:1, (os_ % 2) * 512:(os_ % 2) * 512 + 512],
                              start=False, stop=True)
                bump(TG, mm)
            tg_g = TG.n

            def emit_evac(os_=os_, tt=tt, ps=ps, st=st, g_=g_, tg_g=tg_g, css=css,
                          dram_out=dram_out, stg_slot=stg_slot, pre_ve=pre_ve):
                @V
                def _(v):
                    v.wait_ge(TG.h, tg_g)
                    if g_ >= 1:
                        ps_, pv = cs_stg[g_ - 1]
                        v.wait_ge(ps_.h, pv)
                    if pre_ve is not None:
                        v.wait_ge(VE.h, pre_ve)
                    i = v.tensor_copy(st[:, :] if stg_slot is None else st, ps[:, :])
                    bump(VE, i)
                    i = v.bn_stats(afbn[:, (tt * NOS + os_) * 6:(tt * NOS + os_ + 1) * 6],
                                   ps[:, :])
                    bump(VE, i)
                ve_ev[g_] = VE.n

                if stg_slot is None:
                    @A
                    def _(a):
                        a.wait_ge(VE.h, ve_ev[g_])
                        a.dma_start(dram_out[tt * 128:(tt + 1) * 128,
                                             os_ * OSW:(os_ + 1) * OSW],
                                    st[:, :]).then_inc(css.h, 16)
                    css.n += 16
                    cs_stg[g_] = (css, css.n)
                else:
                    cs_stg[g_] = (VE, VE.n)

            gidx[0] += 1
            if tt == NT - 1:
                tg_os_done[(key, os_)] = TG.n
            if defer is not None:
                defer.append(emit_evac)
            else:
                emit_evac()
            if os_ == NOS - 1 and stats_fin is not None:
                stats_fin(tt)

        def stats_fin_maker(mu, rst, ac_done):
            def fin(tt):
                @V
                def _(v, tt=tt):
                    v.wait_ge(VE.h, VE.n)
                    i = v.bn_aggr(agg8[:, (tt % 2) * 2:(tt % 2) * 2 + 2],
                                  afbn[:, tt * NOS * 6:(tt + 1) * NOS * 6])
                    bump(VE, i)
                    v.wait_ge(VE.h, VE.n)
                    i = v.tensor_copy(mu[:, tt:tt + 1], agg8[:, (tt % 2) * 2:(tt % 2) * 2 + 1])
                    bump(VE, i)
                    v.wait_ge(VE.h, VE.n)
                    i = v.tensor_scalar(sgl8[:, tt * 3:tt * 3 + 1],
                                        agg8[:, (tt % 2) * 2 + 1:(tt % 2) * 2 + 2], EPS, None, ALU.add)
                    bump(VE, i)
                ve_v = VE.n

                @A
                def _(a, tt=tt, ve_v=ve_v):
                    a.wait_ge(VE.h, ve_v)
                    i = a.activation(sgl8[:, tt * 3 + 1:tt * 3 + 2],
                                     sgl8[:, tt * 3:tt * 3 + 1], AF.Ln, scale=1.0)
                    bump(AC, i)
                    a.wait_ge(AC.h, AC.n)
                    i = a.activation(rst[:, tt:tt + 1], sgl8[:, tt * 3 + 1:tt * 3 + 2],
                                     AF.Exp, scale=-0.5)
                    bump(AC, i)
                ac_done[tt] = AC.n
            return fin

        def p3_guard(os_):
            def guard(g, os_=os_):
                if os_ <= 1:
                    s_, sv = stg_free[2 + os_]
                    g.wait_ge(s_.h, sv)
                else:
                    g.wait_ge(TG.h, tg_os_done[("m", os_ - 2)])
            return guard

        ac_astat = {}
        a_fin = stats_fin_maker(amu, arst, ac_astat)

        for ot in range(4):
            emit_mask_tile(ot)
        emit_x_tile(0)
        emit_slab_load("m", wmT, 0, p3_guard(0))
        emit_slab_load("m", wmT, 1, p3_guard(1))

        @A
        def _(a):
            for ot in range(4):
                s_, sv = tree_done[ot]
                a.wait_ge(s_.h, sv)
            a.dma_start(btin[:, :], BTf[:, 0:16]).then_inc(CCS.h, 16)
        CCS.n += 16

        @G
        def _(g):
            g.wait_ge(CCS.h, 16)
            g.collective_compute(
                "AllGather", ALU.bypass,
                replica_groups=[list(range(NCORE))],
                ins=[btin.ap().opt()],
                outs=[btout.ap().opt()],
            ).then_inc(CCD.h, 1)
            g.wait_ge(CCD.h, 1)
            src_ap = bass.AP(btout, 0, [[16, 128], [128 * 16, 8], [1, 16]])
            dst_ap = bass.AP(BTf, 0, [[128, 128], [16, 8], [1, 16]])
            g.dma_start(dst_ap, src_ap).then_inc(LGC.h, 16)
        CCD.n += 1
        LGC.n += 16
        lg_bt = LGC.n

        for _tt in range(1, 6):
            emit_x_tile(_tt)

        @V
        def _(v):
            v.wait_ge(LGC.h, lg_bt)
            c = v.tensor_scalar(Hp[:, 0:128], BTf[:, :], 0.0, 1.0, ALU.max, ALU.min)
            bump(VE, c)
        ve_btb = VE.n

        ve_pb = {}
        for kc in range(KC):
            pss = psH[kc % 2]

            @T
            def _(t, kc=kc, pss=pss):
                t.wait_ge(VE.h, ve_btb)
                if kc >= 2:
                    t.wait_ge(VE.h, ve_pb[kc - 2])
                mm = t.matmul(pss[:, 0:128], r2[:, kc * 128:(kc + 1) * 128],
                              Hp[:, 0:128], start=True, stop=True)
                bump(TT, mm)
            tt_pb = TT.n

            @V
            def _(v, kc=kc, tt_pb=tt_pb, pss=pss):
                v.wait_ge(TT.h, tt_pb)
                c = v.tensor_copy(r1[:, kc * 128:(kc + 1) * 128], pss[:, 0:128])
                bump(VE, c)
            ve_pb[kc] = VE.n
        ve_pbexp_done = VE.n
        tt_sel_end = TT.n

        emit_mask_mult(0, split=True)
        emit_mask_mult(1, split=True)
        pend = []
        for tt in range(NT):
            if tt + 6 < NT:
                emit_x_tile(tt + 6)
            for e in pend:
                e()
            pend = []
            emit_group("m", 0, tt, None, a_fin, a_dram, ac_h_done, defer=pend)
            emit_group("m", 1, tt, None, a_fin, a_dram, ac_h_done, defer=pend)
        for e in pend:
            e()
        pend = []
        for os_ in range(2, NOS):
            emit_slab_load("m", wmT, os_, p3_guard(os_))
            emit_mask_mult(os_)
            for tt in range(NT):
                emit_group("m", os_, tt, None, a_fin, a_dram, ac_h_done)
        tg_p3_end = TG.n
        css_p3 = (CSS[0].n, CSS[1].n)
        csf_p3 = [CSF[i].n for i in range(NT)]

        @G
        def _(g):
            g.wait_ge(TG.h, tg_p3_end)
            g.wait_ge(TT.h, tt_sel_end)
            g.dma_start(r1[:, :], ga[:, :]).then_inc(LGC.h, 16)
            g.dma_start(r2[:, :], bcat[:, :]).then_inc(LGC.h, 16)
        LGC.n += 32
        lg_ffn = LGC.n

        def p6_guard(os_):
            def guard(g, os_=os_):
                if os_ <= 1:
                    g.wait_ge(TG.h, tg_p3_end)
                else:
                    g.wait_ge(TG.h, tg_os_done[("b", os_ - 2)])
            return guard

        emit_slab_load("b", wbT, 0, p6_guard(0))
        emit_slab_load("b", wbT, 1, p6_guard(1))

        ve_lna = {}
        ve_h2ap = {}
        cs_x2 = {}
        ve_smx_done = {}
        ac_h2_done = {}
        po_x2 = {}
        p5_state = {}

        def p5_stage_a(tt):
            xb = xbuf[tt % 2]
            lsx = LSX[tt % 2]
            tp = tmp[tt % 2]

            @S
            def _(s, tt=tt, xb=xb, lsx=lsx):
                if tt >= 2:
                    cx = cs_x2[tt - 2]
                    s.wait_ge(cx[0].h, cx[1])
                    s.wait_ge(VE.h, ve_h2ap[tt - 2])
                else:
                    s.wait_ge(VE.h, ve_xfree[NT - 2 + tt])
                s.dma_start(xb[:, :], xin[tt * 128:(tt + 1) * 128, :]).then_inc(lsx.h, 16)
                if tt >= 1:
                    s.wait_ge(VE.h, ve_lna[tt - 1])
                s.wait_ge(CSS[0].h, css_p3[0])
                s.wait_ge(CSF[tt].h, csf_p3[tt])
                s.dma_start(afin[:, :], a_dram[tt * 128:(tt + 1) * 128, :]).then_inc(LSA.h, 16)
            lsx.n += 16
            LSA.n += 16
            ls_x, ls_a = lsx.n, LSA.n

            @V
            def _(v, tt=tt, tp=tp, ls_a=ls_a, guards=tuple(tmp_guard[tt % 2])):
                v.wait_ge(LSA.h, ls_a)
                v.wait_ge(AC.h, ac_astat[tt])
                for (sh, sv) in guards:
                    v.wait_ge(sh, sv)
                i = v.tensor_scalar(tp[:, :], afin[:, :], amu[:, tt:tt + 1],
                                    arst[:, tt:tt + 1], ALU.subtract, ALU.mult)
                bump(VE, i)
            ve_lna[tt] = VE.n

            @G
            def _(g, tt=tt, xb=xb, tp=tp, lsx=lsx, ls_x=ls_x):
                g.wait_ge(lsx.h, ls_x)
                g.wait_ge(VE.h, ve_lna[tt])
                i = g.tensor_tensor(xb[:, 0:768], xb[:, 0:768], tp[:, 0:768], ALU.add)
                bump(PO, i)
            po_x2[tt] = PO.n

            @V
            def _(v, tt=tt, xb=xb, tp=tp, lsx=lsx, ls_x=ls_x):
                v.wait_ge(lsx.h, ls_x)
                v.wait_ge(VE.h, ve_lna[tt])
                i = v.tensor_tensor(xb[:, 768:4096], xb[:, 768:4096],
                                    tp[:, 768:4096], ALU.add)
                bump(VE, i)
            ve_x2b = VE.n

            csx = CSX[tt % 2]

            @A
            def _(a, tt=tt, xb=xb, csx=csx, ve_x2b=ve_x2b):
                a.wait_ge(PO.h, po_x2[tt])
                a.wait_ge(VE.h, ve_x2b)
                a.dma_start(x2d[tt * 128:(tt + 1) * 128, :], xb[:, :]).then_inc(csx.h, 16)
            csx.n += 16
            cs_x2[tt] = (csx, csx.n)

            @V
            def _(v, tt=tt, xb=xb, ve_x2b=ve_x2b):
                v.wait_ge(PO.h, po_x2[tt])
                v.wait_ge(VE.h, ve_x2b)
                sb0 = (tt % 2) * 48
                for c in range(8):
                    i = v.bn_stats(stat8[:, sb0 + c * 6:sb0 + (c + 1) * 6],
                                   xb[:, c * 512:(c + 1) * 512])
                    bump(VE, i)
                v.wait_ge(VE.h, VE.n)
                i = v.bn_aggr(agg8[:, (tt % 2) * 2:(tt % 2) * 2 + 2], stat8[:, sb0:sb0 + 48])
                bump(VE, i)
                v.wait_ge(VE.h, VE.n)
                i = v.tensor_scalar(sgl8[:, tt * 3:tt * 3 + 1],
                                    agg8[:, (tt % 2) * 2 + 1:(tt % 2) * 2 + 2], EPS, None, ALU.add)
                bump(VE, i)
            ve_var2 = VE.n

            @A
            def _(a, tt=tt, ve_var2=ve_var2):
                a.wait_ge(VE.h, ve_var2)
                i = a.activation(sgl8[:, tt * 3 + 1:tt * 3 + 2],
                                 sgl8[:, tt * 3:tt * 3 + 1], AF.Ln, scale=1.0)
                bump(AC, i)
                a.wait_ge(AC.h, AC.n)
                i = a.activation(sgl8[:, tt * 3 + 2:tt * 3 + 3],
                                 sgl8[:, tt * 3 + 1:tt * 3 + 2], AF.Exp, scale=-0.5)
                bump(AC, i)
            ac_r2 = AC.n

            @V
            def _(v, tt=tt, xb=xb, tp=tp, ac_r2=ac_r2):
                v.wait_ge(AC.h, ac_r2)
                v.wait_ge(PO.h, po_x2[tt])
                i = v.tensor_scalar(tp[:, :], xb[:, :], agg8[:, (tt % 2) * 2:(tt % 2) * 2 + 1],
                                    sgl8[:, tt * 3 + 2:tt * 3 + 3],
                                    ALU.subtract, ALU.mult)
                bump(VE, i)
            ve_h2ap[tt] = VE.n
            ve_h2 = VE.n

            for b4 in range(8):
                kc0 = b4 * 4
                srcs = [tp[:, (kc0 + j) * 128:(kc0 + j + 1) * 128] for j in range(4)]

                def writer(a, ps, tt=tt, kc0=kc0):
                    out_ap = bass.AP(hbuf, kc0 * 1024 + tt * 128,
                                     [[hbuf.shape[1], 128], [1024, 4], [1, 128]])
                    in_ap = bass.AP(ps, 0, [[ps.shape[1], 128], [128, 4], [1, 128]])
                    if hasattr(a, "copy"):
                        return a.copy(out_ap, in_ap)
                    return a.tensor_copy(out_ap, in_ap)

                transpose_batch(srcs, writer, [(VE.h, ve_h2)],
                                evac_engine=("V" if b4 % 2 else None))
            tmp_guard[tt % 2] = [(TT.h, TT.n)]
            ac_h2_done[tt] = [(AC.h, AC.n), (VE.h, VE.n)]

        def p5_stage_b(tt):
            sp = (tt % 2) * 28
            hp = Hp[:, (tt % 2) * 128:(tt % 2) * 128 + 128]
            psh = psH[tt % 2]

            @T
            def _(t, tt=tt, psh=psh):
                for (sh, sv) in ac_h2_done[tt]:
                    t.wait_ge(sh, sv)
                t.wait_ge(LGC.h, lg_ffn)
                t.wait_ge(ONE.h, 1)
                if tt >= 2:
                    t.wait_ge(VE.h, ve_smx_done[tt - 2])
                for kc in range(KC):
                    t.matmul(psh[:, :],
                             hbuf[:, kc * 1024 + tt * 128:kc * 1024 + (tt + 1) * 128],
                             r1[:, kc * 144:(kc + 1) * 144],
                             start=(kc == 0), stop=False)
                mm = t.matmul(psh[:, :], onesb[:, :], gabt[0:1, :], start=False, stop=True)
                bump(TG, mm)
            tg_gate = TG.n

            @V
            def _(v, tt=tt, psh=psh, tg_gate=tg_gate):
                v.wait_ge(TG.h, tg_gate)
                i = v.reduce_max(nmb[:, (tt % 2):(tt % 2) + 1], psh[:, 128:136],
                                 axis=AXX, negate=True)
                bump(VE, i)
            ve_nm = VE.n

            @A
            def _(a, tt=tt, sp=sp, psh=psh, ve_nm=ve_nm):
                a.wait_ge(VE.h, ve_nm)
                i = a.activation(smx[:, sp + 1:sp + 9], psh[:, 128:136], AF.Exp,
                                 bias=nmb[:, (tt % 2):(tt % 2) + 1], scale=1.0)
                bump(AC, i)
                a.wait_ge(AC.h, AC.n)
                i = a.activation(smx[:, sp + 9:sp + 10], psh[:, 136:137], AF.Exp, scale=-1.0)
                bump(AC, i)
            ac_exp = AC.n

            @V
            def _(v, tt=tt, sp=sp, hp=hp, psh=psh, ac_exp=ac_exp):
                v.wait_ge(AC.h, ac_exp)
                i = v.reduce_sum(smx[:, sp + 10:sp + 11], smx[:, sp + 1:sp + 9], axis=AXX); bump(VE, i)
                v.wait_ge(VE.h, VE.n)
                i = v.reciprocal(smx[:, sp + 11:sp + 12], smx[:, sp + 10:sp + 11]); bump(VE, i)
                i = v.tensor_scalar(smx[:, sp + 12:sp + 13], smx[:, sp + 9:sp + 10],
                                    1.0, None, ALU.add); bump(VE, i)
                v.wait_ge(VE.h, VE.n)
                i = v.reciprocal(smx[:, sp + 13:sp + 14], smx[:, sp + 12:sp + 13]); bump(VE, i)
                v.wait_ge(VE.h, VE.n)
                i = v.tensor_scalar(smx[:, sp + 14:sp + 15], smx[:, sp + 13:sp + 14],
                                    MAXTHR, None, ALU.mult); bump(VE, i)
                v.wait_ge(VE.h, VE.n)
                i = v.tensor_scalar(smx[:, sp + 16:sp + 24], smx[:, sp + 1:sp + 9],
                                    smx[:, sp + 11:sp + 12], smx[:, sp + 14:sp + 15],
                                    ALU.mult, ALU.subtract); bump(VE, i)
                v.wait_ge(VE.h, VE.n)
                i = v.tensor_scalar(smx[:, sp + 16:sp + 24], smx[:, sp + 16:sp + 24],
                                    0.0, None, ALU.max); bump(VE, i)
                v.wait_ge(VE.h, VE.n)
                i = v.reduce_sum(smx[:, sp + 24:sp + 25], smx[:, sp + 16:sp + 24], axis=AXX); bump(VE, i)
                v.wait_ge(VE.h, VE.n)
                i = v.tensor_scalar(nmb[:, 2 + tt % 2:3 + tt % 2], smx[:, sp + 24:sp + 25],
                                    0.0, None, ALU.is_equal); bump(VE, i)
                v.wait_ge(VE.h, VE.n)
                i = v.tensor_tensor(smx[:, sp + 26:sp + 27], smx[:, sp + 24:sp + 25],
                                    nmb[:, 2 + tt % 2:3 + tt % 2], ALU.add); bump(VE, i)
                v.wait_ge(VE.h, VE.n)
                i = v.reciprocal(smx[:, sp + 27:sp + 28], smx[:, sp + 26:sp + 27]); bump(VE, i)
                v.wait_ge(VE.h, VE.n)
                i = v.tensor_scalar(smx[:, sp + 16:sp + 24], smx[:, sp + 16:sp + 24],
                                    smx[:, sp + 27:sp + 28], SCALING,
                                    ALU.mult, ALU.mult); bump(VE, i)
                v.wait_ge(VE.h, VE.n)
                for e in range(8):
                    i = v.tensor_scalar(hp[:, e * 16:(e + 1) * 16], psh[:, e * 16:(e + 1) * 16],
                                        smx[:, sp + 16 + e:sp + 17 + e], None, ALU.mult)
                    bump(VE, i)
            ve_hp = VE.n

            def writerH(a, ps, tt=tt):
                return a.copy(HT[:, tt * 128:(tt + 1) * 128], ps[:, 0:128])

            transpose_batch([hp], writerH, [(VE.h, ve_hp)])

            @V
            def _(v):
                i = v.engine_nop()
                bump(VE, i)
            ve_smx_done[tt] = VE.n
            ac_h2_done[tt] = [(AC.h, AC.n), (VE.h, VE.n)]

            def moe0(t, ps, tt=tt):
                t.matmul(ps[:, :], HT[:, tt * 128:(tt + 1) * 128],
                         r2[:, 0:OSW], start=False, stop=False)
            emit_group("b", 0, tt, moe0, None, f_dram, ac_h2_done, defer=p5_pend)

            def moe1(t, ps, tt=tt):
                t.matmul(ps[:, :], HT[:, tt * 128:(tt + 1) * 128],
                         r2[:, OSW:2 * OSW], start=False, stop=False)
            emit_group("b", 1, tt, moe1, None, f_dram, ac_h2_done, defer=p5_pend)

        p5_pend = []
        for tt in range(NT + 1):
            for e in list(p5_pend):
                e()
            p5_pend.clear()
            if tt >= 1:
                p5_stage_b(tt - 1)
            if tt < NT:
                p5_stage_a(tt)
        for e in list(p5_pend):
            e()
        p5_pend.clear()

        csx_p5 = (CSX[0].n, CSX[1].n)

        ac_fstat = {}
        f_fin = stats_fin_maker(fmu, frst, ac_fstat)
        ve_lnf = {}
        cs_out = {}
        po_out = {}

        ve_add = {}
        cs_out = {}

        def emit_p8_tile(tt):
            xb = xbuf[tt % 2]
            lsx = LSX[tt % 2]
            tp = tmp[tt % 2]
            css0_v = CSS[0].n
            lsf = LSF[tt % 2]
            W7 = (NOS - 1) * OSW
            fbuf = slab[0][:, (tt % 2) * D:(tt % 2) * D + W7]

            @S
            def _(s, tt=tt, xb=xb, lsx=lsx, css0_v=css0_v,
                  lsf=lsf, fbuf=fbuf, W7=W7):
                s.wait_ge(CSS[0].h, css0_v)
                s.wait_ge(CSX[0].h, csx_p5[0])
                s.wait_ge(CSX[1].h, csx_p5[1])
                if tt >= 2:
                    co = cs_out[tt - 2]
                    s.wait_ge(co[0].h, co[1])
                s.dma_start(xb[:, :], x2d[tt * 128:(tt + 1) * 128, :]).then_inc(lsx.h, 16)
                s.wait_ge(TG.h, tg_os_done[("b", NOS - 2)])
                if tt >= 2:
                    s.wait_ge(VE.h, ve_lnf[tt - 2])
                s.dma_start(fbuf, f_dram[tt * 128:(tt + 1) * 128, 0:W7]).then_inc(lsf.h, 16)
            lsx.n += 16
            lsf.n += 16
            ls_x, ls_f = lsx.n, lsf.n

            st7, ve7 = stg7[tt]

            @V
            def _(v, tt=tt, tp=tp, ls_f=ls_f, lsf=lsf, fbuf=fbuf, W7=W7,
                  st7=st7, ve7=ve7, guards=tuple(tmp_guard[tt % 2])):
                v.wait_ge(lsf.h, ls_f)
                v.wait_ge(AC.h, ac_fstat[tt])
                v.wait_ge(VE.h, ve7)
                for (sh, sv) in guards:
                    v.wait_ge(sh, sv)
                i = v.tensor_scalar(tp[:, 0:W7], fbuf, fmu[:, tt:tt + 1],
                                    frst[:, tt:tt + 1], ALU.subtract, ALU.mult)
                bump(VE, i)
                i = v.tensor_scalar(tp[:, W7:D], st7, fmu[:, tt:tt + 1],
                                    frst[:, tt:tt + 1], ALU.subtract, ALU.mult)
                bump(VE, i)
            ve_lnf[tt] = VE.n

            @G
            def _(g, tt=tt, xb=xb, tp=tp, lsx=lsx, ls_x=ls_x):
                g.wait_ge(lsx.h, ls_x)
                g.wait_ge(VE.h, ve_lnf[tt])
                i = g.tensor_tensor(xb[:, 0:768], xb[:, 0:768], tp[:, 0:768], ALU.add)
                bump(PO, i)
            po_out[tt] = PO.n

            @V
            def _(v, tt=tt, xb=xb, tp=tp, lsx=lsx, ls_x=ls_x):
                v.wait_ge(lsx.h, ls_x)
                v.wait_ge(VE.h, ve_lnf[tt])
                i = v.tensor_tensor(xb[:, 768:4096], xb[:, 768:4096],
                                    tp[:, 768:4096], ALU.add)
                bump(VE, i)
            ve_add[tt] = VE.n
            tmp_guard[tt % 2] = [(PO.h, PO.n), (VE.h, ve_add[tt])]

            csx = CSX[tt % 2]

            @A
            def _(a, tt=tt, xb=xb, csx=csx):
                a.wait_ge(PO.h, po_out[tt])
                a.wait_ge(VE.h, ve_add[tt])
                if tt == NT - 1:
                    a.dma_start(outd[tt * 128:(tt + 1) * 128, 0:2048],
                                xb[:, 0:2048]).then_inc(csx.h, 16)
                else:
                    a.dma_start(outd[tt * 128:(tt + 1) * 128, :],
                                xb[:, :]).then_inc(csx.h, 16)
            csx.n += 16

            if tt == NT - 1:
                @S
                def _(s, tt=tt, xb=xb, csx=csx):
                    s.wait_ge(PO.h, po_out[tt])
                    s.wait_ge(VE.h, ve_add[tt])
                    s.dma_start(outd[tt * 128:(tt + 1) * 128, 2048:4096],
                                xb[:, 2048:4096]).then_inc(csx.h, 16)
                csx.n += 16
            cs_out[tt] = (csx, csx.n)

        stg7 = {}
        slot7 = {}
        pend7 = []

        def flush7(tt_prev):
            for e in pend7:
                e()
            pend7.clear()
            stg7[tt_prev] = (slot7[tt_prev], VE.n)

        for os_ in range(2, NOS):
            emit_slab_load("b", wbT, os_, p6_guard(os_))
            for tt in range(NT):
                def moe(t, ps, os_=os_, tt=tt):
                    t.matmul(ps[:, :], HT[:, tt * 128:(tt + 1) * 128],
                             r2[:, os_ * OSW:(os_ + 1) * OSW], start=False, stop=False)
                if os_ == NOS - 1:
                    if tt >= 1:
                        flush7(tt - 1)
                    slot7[tt] = slab[0][:, 2 * D + (tt % 2) * OSW:
                                        2 * D + (tt % 2) * OSW + OSW]
                    emit_group("b", os_, tt, moe, None, f_dram, ac_h2_done,
                               stg_slot=slot7[tt], pre_ve=ve_lnf.get(tt - 2),
                               defer=pend7)
                else:
                    emit_group("b", os_, tt, moe, f_fin, f_dram, ac_h2_done)
                if os_ == NOS - 1 and tt >= 1:
                    f_fin(tt - 1)
                    emit_p8_tile(tt - 1)
        flush7(NT - 1)
        f_fin(NT - 1)
        emit_p8_tile(NT - 1)
        css_p6 = (CSS[0].n, CSS[1].n)

        @A
        def _(a):
            a.wait_ge(CSX[0].h, CSX[0].n)
            a.wait_ge(CSX[1].h, CSX[1].n)

        @S
        def _(s):
            s.wait_ge(CSX[0].h, CSX[0].n)
            s.wait_ge(CSX[1].h, CSX[1].n)

    return nc


_NC_CACHE = None


def _host_pack(inputs):
    L0_attn_data = np.asarray(inputs["L0_attn_data"], dtype=np.float32)
    mask = np.ascontiguousarray(np.asarray(inputs["L0_attn_mask"], dtype=np.int32))
    weight = np.asarray(inputs["L0_attn_weight"], dtype=np.float32)
    battn = np.asarray(inputs["L0_attn_bias"], dtype=np.float32)
    W_base = np.asarray(inputs["L0_ffn_W_base"], dtype=np.float32)
    b_base = np.asarray(inputs["L0_ffn_b_base"], dtype=np.float32)
    W_gate = np.asarray(inputs["L0_ffn_W_gate"], dtype=np.float32)
    W_thr = np.asarray(inputs["L0_ffn_W_thr"], dtype=np.float32)
    b_thr = np.asarray(inputs["L0_ffn_b_thr"], dtype=np.float32)
    A_w = np.asarray(inputs["L0_ffn_A_w"], dtype=np.float32)
    B_w = np.asarray(inputs["L0_ffn_B_w"], dtype=np.float32)

    x8 = np.ascontiguousarray(L0_attn_data.reshape(NCORE * TPC, D))
    wmT = np.ascontiguousarray(weight.T)
    wbT = np.ascontiguousarray(W_base.T)

    AcatT = A_w.transpose(2, 0, 1).reshape(D, 128)
    gcols = np.concatenate([W_gate.T, W_thr[:, None]], axis=1)
    gap = np.zeros((128, KC * 144), dtype=np.float32)
    for kc in range(KC):
        gap[:, kc * 144:kc * 144 + 128] = AcatT[kc * 128:(kc + 1) * 128, :]
        gap[:, kc * 144 + 128:kc * 144 + 137] = gcols[kc * 128:(kc + 1) * 128, :]
    gab = np.zeros((1, 144), dtype=np.float32)
    gab[0, 136] = b_thr[0]
    bcat = np.ascontiguousarray(B_w.transpose(0, 2, 1).reshape(128, D))

    kb = np.arange(128)[:, None]
    p = np.arange(128)[None, :]
    selp = np.zeros((128, KC * 128), dtype=BF)
    for kc in range(KC):
        selp[:, kc * 128:(kc + 1) * 128] = (kb == 4 * kc + p // 32).astype(BF)
    idnp = np.eye(128, dtype=BF)

    shared = {
        "wmT": wmT, "wbT": wbT, "ga": gap, "gab": gab,
        "bcat": bcat, "battn": np.ascontiguousarray(battn[None, :]),
        "bbase": np.ascontiguousarray(b_base[None, :]),
        "selp": selp, "idnp": idnp,
    }
    in_maps = []
    for c in range(NCORE):
        m = dict(shared)
        m["xin"] = np.ascontiguousarray(x8[c * TPC:(c + 1) * TPC, :])
        m["maskN"] = np.ascontiguousarray(mask[c * (D // 8):(c + 1) * (D // 8), :])
        in_maps.append(m)
    return in_maps


def kernel(**inputs):
    global _NC_CACHE
    in_maps = _host_pack(inputs)
    if _NC_CACHE is None:
        _NC_CACHE = build_nc()
    res = run_bass_kernel_spmd(_NC_CACHE, in_maps, core_ids=list(range(NCORE)))
    outs = [np.asarray(res.results[c]["outd"]) for c in range(NCORE)]
    full = np.concatenate(outs, axis=0).reshape(4, 2048, D).astype(np.float32)
    return full
